# version 05
# baseline (speedup 1.0000x reference)
import numpy as np
import ml_dtypes
from contextlib import ExitStack
import concourse.bass as bass
import concourse.mybir as mybir
from concourse.bass_utils import run_bass_kernel_spmd

F32 = mybir.dt.float32
BF16 = mybir.dt.bfloat16
AF = mybir.ActivationFunctionType
ALU = mybir.AluOpType
AX = mybir.AxisListType.X

NCORES = 8
T_FULL = 8192
TC = T_FULL // NCORES
D = 2048
PART = 128
BF = np.dtype(ml_dtypes.bfloat16)

ENGINES = ("tensor", "vector", "scalar", "gpsimd", "sync")


def cdiv(a, b):
    return -(-a // b)



class DmaGroup:
    def __init__(self, prog, sem_idx, sw):
        self.prog = prog
        self.sem_idx = sem_idx
        self.sw = sw
        self.n = 0
        self.sealed = False

    def dma(self, queue, emit, deps=()):
        assert not self.sealed
        sw = queue == "gpsimd"
        assert sw == self.sw, f"dma group queue class mismatch ({queue})"
        st = self.prog.dma_state[self.sem_idx]
        deps = list(deps)
        if self.n == 0 and st["gate"] is not None:
            deps.append(st["gate"])
        self.n += 1
        st["val"] += 16
        self.prog._raw_op(queue, emit, deps, ("dma", self.sem_idx))

    def seal(self):
        assert not self.sealed and self.n > 0
        self.sealed = True
        st = self.prog.dma_state[self.sem_idx]
        tok = ("dma", self.sem_idx, st["val"])
        st["gate"] = tok
        st["open"] = False
        return tok


class Prog:

    N_DMA_SEMS = 14

    def __init__(self, nc, stack):
        self.nc = nc
        self.sems = {e: stack.enter_context(nc.semaphore(f"sem_{e}")) for e in ENGINES}
        self.dma_sems = [stack.enter_context(nc.semaphore(f"dmas{i}"))
                         for i in range(self.N_DMA_SEMS)]
        self.counts = {e: 0 for e in ENGINES}
        self.dma_state = [{"val": 0, "open": False, "gate": None}
                          for _ in range(self.N_DMA_SEMS)]
        self.ops = {e: [] for e in ENGINES}
        self._dma_rr_hw = 0
        self._dma_rr_sw = 0
        self._cc_rr = 0

    def _raw_op(self, engine, emit, deps, inc_spec):
        waits = [d for d in deps if d is not None]
        self.ops[engine].append((emit, waits, inc_spec))

    def op(self, engine, emit, deps=(), inc=True):
        self._raw_op(engine, emit, deps, ("eng", engine) if inc else None)
        if inc:
            self.counts[engine] += 1
            return ("eng", engine, self.counts[engine])
        return None

    def cc(self, emit, deps=()):
        for k in range(2):
            i = 12 + (self._cc_rr + k) % 2
            st = self.dma_state[i]
            if not st["open"]:
                self._cc_rr = (i - 12) + 1
                deps2 = [d for d in deps if d is not None]
                if st["gate"] is not None:
                    deps2.append(st["gate"])
                st["val"] += 1
                self._raw_op("gpsimd", emit, deps2, ("cc", i))
                tok = ("dma", i, st["val"])
                st["gate"] = tok
                return tok
        raise RuntimeError("no free cc sem")

    def dma_percore(self, queue, body_fn, n_dmas, deps=()):
        sw = queue == "gpsimd"
        g = self.dma_group(sw=sw)
        idx = g.sem_idx
        st = self.dma_state[idx]
        deps2 = [d for d in deps if d is not None]
        if st["gate"] is not None:
            deps2.append(st["gate"])
        st["val"] += 16 * n_dmas
        sems = self.dma_sems
        nc = self.nc

        def emit(e):
            for arm in nc.switch_core_id(engines=[e], n=128):
                insts = body_fn(e, arm.logical % 8)
                assert len(insts) == n_dmas, (len(insts), n_dmas)
                for inst in insts:
                    inst.then_inc(sems[idx], 16)
            return None

        self._raw_op(queue, emit, deps2, None)
        g.n = n_dmas
        return g.seal()

    def dma_group(self, sw=False):
        lo, hi = (8, 12) if sw else (0, 8)
        rr = self._dma_rr_sw if sw else self._dma_rr_hw
        n = hi - lo
        for k in range(n):
            i = lo + (rr + k) % n
            if not self.dma_state[i]["open"]:
                if sw:
                    self._dma_rr_sw = (i - lo) + 1
                else:
                    self._dma_rr_hw = (i - lo) + 1
                self.dma_state[i]["open"] = True
                return DmaGroup(self, i, sw)
        raise RuntimeError("all dma sems open; seal some groups")

    def dma1(self, queue, emit, deps=()):
        g = self.dma_group(sw=(queue == "gpsimd"))
        g.dma(queue, emit, deps)
        return g.seal()

    def barrier(self):
        snap = [("eng", e, v) for e, v in self.counts.items() if v > 0]
        snap += [("dma", i, st["val"]) for i, st in enumerate(self.dma_state)
                 if st["val"] > 0]
        for e in ENGINES:
            deps = [t for t in snap if not (t[0] == "eng" and t[1] == e)]
            self._raw_op(e, None, deps, None)

    def token(self, engine):
        v = self.counts[engine]
        return ("eng", engine, v) if v > 0 else None

    def _sem_of(self, tok):
        if tok[0] == "eng":
            return self.sems[tok[1]]
        return self.dma_sems[tok[1]]

    def install(self, block):
        for ename in ENGINES:
            ops = self.ops[ename]

            def closure(eng, ops=ops, ename=ename):
                guaranteed = {}
                for emit, waits, inc_spec in ops:
                    for tok in waits:
                        key = (tok[0], tok[1])
                        val = tok[2]
                        if guaranteed.get(key, 0) >= val:
                            continue
                        eng.wait_ge(self._sem_of(tok), val)
                        guaranteed[key] = val
                    if emit is None:
                        continue
                    inst = emit(eng)
                    if inc_spec is not None:
                        if inc_spec[0] == "eng":
                            inst.then_inc(self.sems[ename], 1)
                        elif inc_spec[0] == "cc":
                            inst.then_inc(self.dma_sems[inc_spec[1]], 1)
                        else:
                            inst.then_inc(self.dma_sems[inc_spec[1]], 16)

            getattr(block, ename)(closure)


class PsumPool:
    def __init__(self, prog, stack, n_f32=8):
        nc = prog.nc
        self.banks = [stack.enter_context(nc.psum_tensor(f"psb{i}", [128, 512], F32))
                      for i in range(n_f32)]
        self.war = [None] * n_f32
        self.n = n_f32
        self.free = list(range(n_f32))

    def acquire(self):
        assert self.free, "all PSUM banks held"
        b = self.free.pop(0)
        return self.banks[b], self.war[b], b

    def release(self, b, consumer_tok):
        self.war[b] = consumer_tok
        self.free.append(b)



def load_weight_bf16(prog, stack, name, wt_dram, K, M, g=None):
    nc = prog.nc
    kc = cdiv(K, PART)
    sb = stack.enter_context(nc.sbuf_tensor(name, [PART, kc, M], BF16))
    view = wt_dram.rearrange("(kc p) m -> p kc m", p=PART)
    own = g is None
    if own:
        g = prog.dma_group(sw=True)
    g.dma("gpsimd", lambda e, sb=sb, view=view: e.dma_start(sb[:, :, :], view))
    tok = g.seal() if own else None
    return sb, tok


def gemm(prog, psum, *, M, K, T, lhsT, rhs, evac, tile_t=512,
         lhs_dep=None, rhs_dep=None, extra_k=None, m_done=None):
    nk = cdiv(K, PART)
    nm = cdiv(M, PART)
    nt = cdiv(T, tile_t)
    for mi in range(nm):
        m_w = min(PART, M - mi * PART)
        for ti in range(nt):
            t_w = min(tile_t, T - ti * tile_t)
            bank, war, bi = psum.acquire()
            mm_tok = None
            for ki in range(nk):
                deps = []
                if ki == 0 and war is not None:
                    deps.append(war)
                if lhs_dep is not None:
                    deps.append(lhs_dep(mi, ki))
                if rhs_dep is not None:
                    deps.append(rhs_dep(ki, ti))
                last = (ki == nk - 1) and extra_k is None
                l_ap = lhsT(mi, ki)
                r_ap = rhs(ki, ti)
                mm_tok = prog.op(
                    "tensor",
                    lambda e, o=bank[0:m_w, 0:t_w], l=l_ap, r=r_ap,
                           s=(ki == 0), p=last:
                        e.matmul(o, l, r, start=s, stop=p),
                    deps, inc=last)
            if extra_k is not None:
                mm_tok = extra_k(mi, ti, bank[0:m_w, 0:t_w], [])
            ctok = evac(mi, ti, bank[0:m_w, 0:t_w], mm_tok)
            psum.release(bi, ctok)
            last_mm = mm_tok
        if m_done is not None:
            m_done(mi, last_mm)


def evac_copy(prog, dst_fn, engine="scalar", func=None, bias=0.0, scale=1.0):
    def cb(mi, ti, ps, mm_tok):
        dst = dst_fn(mi, ti)
        if engine == "scalar":
            f = func if func is not None else AF.Copy
            return prog.op("scalar",
                           lambda e, d=dst, p=ps, f=f, b=bias, s=scale:
                               e.activation(d, p, f, bias=b, scale=s),
                           [mm_tok])
        return prog.op("vector",
                       lambda e, d=dst, p=ps: e.tensor_copy(d, p),
                       [mm_tok])
    return cb



class Misc:
    def __init__(self, prog, stack, T):
        nc = prog.nc
        self.T = T
        self.ones_col = stack.enter_context(nc.sbuf_tensor("ones_col", [PART, 1], F32))
        self.ones_row = stack.enter_context(nc.sbuf_tensor("ones_row", [1, PART], F32))
        self.sq = stack.enter_context(nc.sbuf_tensor("sq_scr", [PART, 2, 512], F32))
        self.sq_war = [None, None]
        self.row_a = stack.enter_context(nc.sbuf_tensor("row_a", [1, T], F32))
        self.row_b = stack.enter_context(nc.sbuf_tensor("row_b", [1, T], F32))
        self.row_c = stack.enter_context(nc.sbuf_tensor("row_c", [1, T], F32))
        self.bc_m = stack.enter_context(nc.sbuf_tensor("bc_m", [PART, T], F32))
        self.bc_r = stack.enter_context(nc.sbuf_tensor("bc_r", [PART, T], F32))
        self.tmp = stack.enter_context(nc.sbuf_tensor("ln_tmp", [PART, T], F32))
        self.cpool = stack.enter_context(nc.sbuf_tensor("cpool", [PART, 8], F32))
        self.cmap = {}
        self.prog = prog
        self.tmp_war = None
        self.norm_war = None
        t1 = prog.op("vector", lambda e: e.memset(self.ones_col[:, :], 1.0))
        t2 = prog.op("vector", lambda e: e.memset(self.ones_row[:, :], 1.0))
        self.init_tok = t2

    def cc(self, value, rows=PART):
        if value not in self.cmap:
            i = len(self.cmap)
            assert i < 8
            tok = self.prog.op("vector",
                               lambda e, i=i, v=value:
                                   e.memset(self.cpool[:, i:i + 1], float(v)))
            self.cmap[value] = (i, tok)
        i, tok = self.cmap[value]
        return self.cpool[0:rows, i:i + 1], tok


def layernorm(prog, psum, misc, *, x, nch, T, out_f32=None, out_bf16=None,
              gamma=None, beta=None, eps=1e-5, use_mean=True, dep=None):
    nt = cdiv(T, 512)
    inv_d = 1.0 / (nch * PART)
    for ti in range(nt):
        t0, t1 = ti * 512, min(T, (ti + 1) * 512)
        tw = t1 - t0
        bank2, war2, bi2 = psum.acquire()
        mm2 = None
        for ch in range(nch):
            sl = ch % 2
            sq_tok = prog.op(
                "scalar",
                lambda e, o=misc.sq[:, sl, 0:tw], i=x[:, ch, t0:t1]:
                    e.activation(o, i, AF.Square),
                [dep, misc.sq_war[sl]])
            mm2 = prog.op(
                "tensor",
                lambda e, o=bank2[0:1, 0:tw], l=misc.ones_col[:, :],
                       r=misc.sq[:, sl, 0:tw], s=(ch == 0), p=(ch == nch - 1):
                    e.matmul(o, l, r, start=s, stop=p),
                [sq_tok, misc.init_tok] + ([war2] if ch == 0 else []),
                inc=True)
            misc.sq_war[sl] = mm2
        t_m2 = prog.op("vector",
                       lambda e, o=misc.row_b[0:1, t0:t1], i=bank2[0:1, 0:tw]:
                           e.tensor_scalar_mul(o, i, inv_d),
                       [mm2, misc.norm_war])
        psum.release(bi2, t_m2)
        if use_mean:
            bank1, war1, bi1 = psum.acquire()
            mm1 = None
            for ch in range(nch):
                mm1 = prog.op(
                    "tensor",
                    lambda e, o=bank1[0:1, 0:tw], l=misc.ones_col[:, :],
                           r=x[:, ch, t0:t1], s=(ch == 0), p=(ch == nch - 1):
                        e.matmul(o, l, r, start=s, stop=p),
                    [dep, misc.init_tok] + ([war1] if ch == 0 else []),
                    inc=(ch == nch - 1))
            t_m = prog.op("vector",
                          lambda e, o=misc.row_a[0:1, t0:t1], i=bank1[0:1, 0:tw]:
                              e.tensor_scalar_mul(o, i, inv_d),
                          [mm1, misc.norm_war])
            psum.release(bi1, t_m)
            t_msq = prog.op("vector",
                            lambda e, o=misc.row_c[0:1, t0:t1],
                                   a=misc.row_a[0:1, t0:t1]:
                                e.tensor_mul(o, a, a), [t_m])
            t_var = prog.op("vector",
                            lambda e, o=misc.row_c[0:1, t0:t1],
                                   a=misc.row_b[0:1, t0:t1],
                                   b=misc.row_c[0:1, t0:t1]:
                                e.tensor_sub(o, a, b), [t_msq, t_m2])
        else:
            t_var = t_m2
    var_row = misc.row_c if use_mean else misc.row_b
    eps_ap, eps_tok = misc.cc(float(eps), rows=1)
    t_ln = prog.op("scalar",
                   lambda e, o=var_row[0:1, 0:T], i=var_row[0:1, 0:T],
                          b=eps_ap:
                       e.activation(o, i, AF.Ln, bias=b),
                   [t_var, eps_tok])
    t_rstd = prog.op("scalar",
                     lambda e, o=var_row[0:1, 0:T], i=var_row[0:1, 0:T]:
                         e.activation(o, i, AF.Exp, scale=-0.5),
                     [t_ln])
    toks = []
    for ti in range(nt):
        t0, t1 = ti * 512, min(T, (ti + 1) * 512)
        tw = t1 - t0
        bank, war, bi = psum.acquire()
        mm = prog.op("tensor",
                     lambda e, o=bank[:, 0:tw], l=misc.ones_row[0:1, :],
                            r=var_row[0:1, t0:t1]:
                         e.matmul(o, l, r, start=True, stop=True),
                     [t_rstd, misc.init_tok, war])
        tc_ = prog.op("vector",
                      lambda e, o=misc.bc_r[:, t0:t1], i=bank[:, 0:tw]:
                          e.tensor_copy(o, i), [mm, misc.norm_war])
        psum.release(bi, tc_)
        toks.append(tc_)
        if use_mean:
            bank, war, bi = psum.acquire()
            mm = prog.op("tensor",
                         lambda e, o=bank[:, 0:tw], l=misc.ones_row[0:1, :],
                                r=misc.row_a[0:1, t0:t1]:
                             e.matmul(o, l, r, start=True, stop=True),
                         [t_m, misc.init_tok, war])
            tc2 = prog.op("vector",
                          lambda e, o=misc.bc_m[:, t0:t1], i=bank[:, 0:tw]:
                              e.tensor_copy(o, i), [mm, misc.norm_war])
            psum.release(bi, tc2)
            toks.append(tc2)
    bc_tok = toks[-1]
    bc_all = list(toks)
    out_toks = []
    for ch in range(nch):
        if use_mean:
            t_sub = prog.op("vector",
                            lambda e, o=misc.tmp[:, 0:T], a=x[:, ch, :],
                                   b=misc.bc_m[:, 0:T]:
                                e.tensor_sub(o, a, b),
                            bc_all + [dep, misc.tmp_war])
            src = misc.tmp
            if gamma is not None:
                t_n = prog.op("vector",
                              lambda e, o=misc.tmp[:, 0:T], i=src[:, 0:T],
                                     g=gamma[:, ch:ch + 1], r=misc.bc_r[:, 0:T]:
                                  e.scalar_tensor_tensor(o, i, g, r, ALU.mult, ALU.mult),
                              [t_sub])
                if out_f32 is not None:
                    tk = prog.op("vector",
                                 lambda e, o=out_f32[:, ch, :], i=misc.tmp[:, 0:T],
                                        b=beta[:, ch:ch + 1]:
                                     e.tensor_scalar_add(o, i, b), [t_n])
                    out_toks.append(tk)
                    misc.tmp_war = tk
                    if out_bf16 is not None:
                        tk2 = prog.op("vector",
                                      lambda e, o=out_bf16[:, ch, :], i=out_f32[:, ch, :]:
                                          e.tensor_copy(o, i), [tk])
                        out_toks.append(tk2)
                elif out_bf16 is not None:
                    tk = prog.op("vector",
                                 lambda e, o=out_bf16[:, ch, :], i=misc.tmp[:, 0:T],
                                        b=beta[:, ch:ch + 1]:
                                     e.tensor_scalar_add(o, i, b), [t_n])
                    out_toks.append(tk)
                    misc.tmp_war = tk
            else:
                if out_f32 is not None:
                    tk = prog.op("vector",
                                 lambda e, o=out_f32[:, ch, :], i=misc.tmp[:, 0:T],
                                        r=misc.bc_r[:, 0:T]:
                                     e.tensor_mul(o, i, r), [t_sub])
                    out_toks.append(tk)
                    misc.tmp_war = tk
                    if out_bf16 is not None:
                        tk2 = prog.op("vector",
                                      lambda e, o=out_bf16[:, ch, :], i=out_f32[:, ch, :]:
                                          e.tensor_copy(o, i), [tk])
                        out_toks.append(tk2)
                elif out_bf16 is not None:
                    tk = prog.op("vector",
                                 lambda e, o=out_bf16[:, ch, :], i=misc.tmp[:, 0:T],
                                        r=misc.bc_r[:, 0:T]:
                                     e.tensor_mul(o, i, r), [t_sub])
                    out_toks.append(tk)
                    misc.tmp_war = tk
        else:
            for dst, dt_ in ((out_f32, "f32"), (out_bf16, "bf16")):
                if dst is None:
                    continue
                tk = prog.op("vector",
                             lambda e, o=dst[:, ch, :], i=x[:, ch, :],
                                    g=gamma[:, ch:ch + 1], r=misc.bc_r[:, 0:T]:
                                 e.scalar_tensor_tensor(o, i, g, r, ALU.mult, ALU.mult),
                             bc_all + [dep])
                out_toks.append(tk)
    misc.norm_war = out_toks[-1]
    return out_toks[-1]


class StagePool:
    def __init__(self, prog, stack, name, n, shape, dtype):
        nc = prog.nc
        self.tiles = [stack.enter_context(nc.sbuf_tensor(f"{name}{i}", list(shape), dtype))
                      for i in range(n)]
        self.war = [None] * n
        self.i = 0
        self.n = n

    def acquire(self):
        s = self.i % self.n
        self.i += 1
        return self.tiles[s], self.war[s], s

    def release(self, s, tok):
        self.war[s] = tok


def evac_to_dram(prog, stage, dram_fn, engine="scalar", func=None, bias=0.0,
                 scale=1.0, queue="sync", sink=None):
    def cb(mi, ti, ps, mm_tok):
        tile, war, si = stage.acquire()
        pw, tw = ps.shape[-2], ps.shape[-1]
        dst = tile[0:pw, 0:tw]
        if engine == "scalar":
            f = func if func is not None else AF.Copy
            cp = prog.op("scalar",
                         lambda e, d=dst, p=ps, f=f, b=bias, s=scale:
                             e.activation(d, p, f, bias=b, scale=s),
                         [mm_tok, war])
        else:
            cp = prog.op("vector",
                         lambda e, d=dst, p=ps: e.tensor_copy(d, p),
                         [mm_tok, war])
        dtok = prog.dma1(queue,
                         lambda e, o=dram_fn(mi, ti), i=dst: e.dma_start(o, i),
                         [cp])
        stage.release(si, dtok)
        if sink is not None:
            sink.append(dtok)
        return cp
    return cb


class WStreamer:
    def __init__(self, prog, buf, wt_dram, K, M, mc=256, queue="gpsimd",
                 init_war=None):
        self.prog = prog
        self.K, self.M, self.mc = K, M, mc
        self.kc = cdiv(K, PART)
        assert self.kc <= buf.shape[1] and mc == buf.shape[3]
        self.view = wt_dram.rearrange("(kc p) m -> p kc m", p=PART)
        self.buf = buf
        self.queue = queue
        self.nchunks = cdiv(M, mc)
        self.load_toks = {}
        self.slot_war = [init_war, init_war]
        self.issue(0)
        self.issue(1)

    def issue(self, j):
        if j >= self.nchunks:
            return
        slot = j % 2
        m0 = j * self.mc
        w = min(self.mc, self.M - m0)
        g = self.prog.dma_group(sw=(self.queue == "gpsimd"))
        g.dma(self.queue,
              lambda e, o=self.buf[:, 0:self.kc, slot, 0:w],
                     i=self.view[:, :, m0:m0 + w]:
                  e.dma_start(o, i),
              [self.slot_war[slot]])
        self.load_toks[j] = g.seal()

    def lhsT(self, mi, ki):
        j = (mi * PART) // self.mc
        off = mi * PART - j * self.mc
        m_w = min(PART, self.M - mi * PART)
        return self.buf[:, ki, j % 2, off:off + m_w]

    def dep(self, mi, ki):
        return self.load_toks[(mi * PART) // self.mc]

    def m_done(self, mi, tok):
        j = (mi * PART) // self.mc
        nxt = (mi + 1) * PART
        if nxt >= self.M or (nxt // self.mc) != j:
            self.slot_war[j % 2] = tok
            self.issue(j + 2)


def make_bf16(prog, dst, src, nch, dep):
    toks = []
    for ch in range(nch):
        toks.append(prog.op("vector",
                            lambda e, o=dst[:, ch, :], i=src[:, ch, :]:
                                e.tensor_copy(o, i), [dep]))
    return toks[-1]



def build_launch1():
    nc = bass.Bass("TRN2", target_bir_lowering=False)
    xT = nc.dram_tensor("xT", [D, TC], F32, kind="ExternalInput").ap()
    wq0t = nc.dram_tensor("wq0t", [D, D], F32, kind="ExternalInput").ap()
    wk0t = nc.dram_tensor("wk0t", [D, 512], F32, kind="ExternalInput").ap()
    wv0t = nc.dram_tensor("wv0t", [D, 512], F32, kind="ExternalInput").ap()
    wq1t = nc.dram_tensor("wq1t", [D, D], F32, kind="ExternalInput").ap()
    wk1t = nc.dram_tensor("wk1t", [D, D], F32, kind="ExternalInput").ap()
    wv1t = nc.dram_tensor("wv1t", [D, D], F32, kind="ExternalInput").ap()
    wo1n = nc.dram_tensor("wo1n", [D, D], F32, kind="ExternalInput").ap()
    a_n = nc.dram_tensor("a_n", [D, 16], F32, kind="ExternalInput").ap()
    ln1v = nc.dram_tensor("ln1v", [PART, 16], F32, kind="ExternalInput").ap()
    ln2v = nc.dram_tensor("ln2v", [PART, 16], F32, kind="ExternalInput").ap()

    o_xn = nc.dram_tensor("o_xn", [D, TC], F32, kind="ExternalOutput").ap()
    o_h = nc.dram_tensor("o_h", [D, TC], BF16, kind="ExternalOutput").ap()
    o_q0 = nc.dram_tensor("o_q0", [D, TC], BF16, kind="ExternalOutput").ap()
    o_k0 = nc.dram_tensor("o_k0", [512, TC], BF16, kind="ExternalOutput").ap()
    o_v0 = nc.dram_tensor("o_v0", [512, TC], BF16, kind="ExternalOutput").ap()
    o_q1 = nc.dram_tensor("o_q1", [D, TC], BF16, kind="ExternalOutput").ap()
    o_k1 = nc.dram_tensor("o_k1", [D, TC], BF16, kind="ExternalOutput").ap()
    o_v1 = nc.dram_tensor("o_v1", [D, TC], BF16, kind="ExternalOutput").ap()
    o_g = nc.dram_tensor("o_g", [D, 16], BF16, kind="ExternalOutput").ap()

    stack = ExitStack()
    with stack:
        prog = Prog(nc, stack)
        psum = PsumPool(prog, stack, 8)
        misc = Misc(prog, stack, TC)
        sb = lambda n_, sh, dt: stack.enter_context(nc.sbuf_tensor(n_, sh, dt))

        x_sb = sb("x_sb", [PART, 16, TC], F32)
        act_bf = sb("act_bf", [PART, 16, TC], BF16)
        g1v = sb("g1v", [PART, 16], F32)
        g2v = sb("g2v", [PART, 16], F32)
        wbuf = sb("wbuf", [PART, 16, 2, 256], BF16)

        gl = prog.dma_group()
        gl.dma("sync", lambda e: e.dma_start(
            x_sb[:, :, :], xT.rearrange("(c p) t -> p c t", p=PART)))
        gl.dma("sync", lambda e: e.dma_start(g1v[:, :], ln1v[:, :]))
        gl.dma("sync", lambda e: e.dma_start(g2v[:, :], ln2v[:, :]))
        t_in = gl.seal()

        stage = StagePool(prog, stack, "st", 4, (PART, 512), BF16)

        t_xn = layernorm(prog, psum, misc, x=x_sb, nch=16, T=TC,
                         out_f32=x_sb, out_bf16=act_bf, eps=1e-5,
                         use_mean=True, dep=t_in)
        xn_view = o_xn.rearrange("(c p) t -> p c t", p=PART)
        gxn = prog.dma_group()
        for ch in range(16):
            gxn.dma("scalar", lambda e, ch=ch: e.dma_start(
                xn_view[:, ch, :], x_sb[:, ch, :]), [t_xn])
        t_oxn = gxn.seal()

        def dram_tiles(out_ap, M):
            v = out_ap.rearrange("(c p) t -> p c t", p=PART)
            def f(mi, ti):
                return v[:, mi, ti * 512:(ti + 1) * 512]
            return f

        def rhs_act(ki, ti):
            return act_bf[:, ki, ti * 512:(ti + 1) * 512]

        for name, wt, Mw, out in (("wq0", wq0t, D, o_q0),
                                  ("wk0", wk0t, 512, o_k0),
                                  ("wv0", wv0t, 512, o_v0)):
            ws = WStreamer(prog, wbuf, wt, D, Mw, init_war=prog.token("tensor"))
            gemm(prog, psum, M=Mw, K=D, T=TC,
                 lhsT=ws.lhsT, rhs=rhs_act, lhs_dep=ws.dep,
                 rhs_dep=lambda ki, ti, t=t_xn: t,
                 evac=evac_to_dram(prog, stage, dram_tiles(out, Mw)),
                 m_done=ws.m_done)

        wso1 = WStreamer(prog, wbuf, wo1n, D, D, init_war=prog.token("tensor"))
        a_sb, t_a = load_weight_bf16(prog, stack, "a_sb", a_n, D, 16)
        g_view = o_g.rearrange("(c p) s -> p c s", p=PART)
        g_stage = StagePool(prog, stack, "gst", 2, (PART, 16), BF16)

        def g_evac(mi, ti, ps, mm_tok):
            tile, war, si = g_stage.acquire()
            cp = prog.op("scalar",
                         lambda e, d=tile[:, 0:16], p=ps: e.activation(d, p, AF.Copy),
                         [mm_tok, war])
            dtok = prog.dma1("sync",
                             lambda e, o=g_view[:, mi, :], i=tile[:, 0:16]:
                                 e.dma_start(o, i), [cp])
            g_stage.release(si, dtok)
            return cp

        gemm(prog, psum, M=D, K=D, T=16, tile_t=16,
             lhsT=wso1.lhsT, rhs=lambda ki, ti: a_sb[:, ki, :],
             lhs_dep=wso1.dep,
             rhs_dep=lambda ki, ti, t=t_a: t,
             evac=g_evac, m_done=wso1.m_done)

        prog.barrier()
        t_x1 = layernorm(prog, psum, misc, x=x_sb, nch=16, T=TC,
                         out_bf16=act_bf, gamma=g1v, eps=1e-6,
                         use_mean=False, dep=t_oxn)
        for name, wt, out in (("wq1", wq1t, o_q1), ("wk1", wk1t, o_k1),
                              ("wv1", wv1t, o_v1)):
            ws = WStreamer(prog, wbuf, wt, D, D, init_war=prog.token("tensor"))
            gemm(prog, psum, M=D, K=D, T=TC,
                 lhsT=ws.lhsT, rhs=rhs_act, lhs_dep=ws.dep,
                 rhs_dep=lambda ki, ti, t=t_x1: t,
                 evac=evac_to_dram(prog, stage, dram_tiles(out, D)),
                 m_done=ws.m_done)

        prog.barrier()
        t_h = layernorm(prog, psum, misc, x=x_sb, nch=16, T=TC,
                        out_bf16=act_bf, gamma=g2v, eps=1e-6,
                        use_mean=False, dep=None)
        h_view = o_h.rearrange("(c p) t -> p c t", p=PART)
        gh = prog.dma_group()
        for ch in range(16):
            gh.dma("sync", lambda e, ch=ch: e.dma_start(
                h_view[:, ch, :], act_bf[:, ch, :]), [t_h])
        t_oh = gh.seal()

        prog.barrier()
        block = stack.enter_context(nc.Block())
        prog.install(block)
    return nc



WINDOW = 1024
SINK = 4
NEG = -1.0e9


def _vis_window(i, j):
    causal = j <= i
    band = j >= i - (WINDOW - 1)
    vis = (causal & band) | (j < SINK)
    vis &= (i >= SINK) | causal
    return vis


def _vis_causal(i, j):
    return j <= i


def mask_patterns(kind):
    vis_fn = _vis_window if kind == "window" else _vis_causal
    klists, tilemap, patterns, phash = {}, {}, [], {}
    for qt in range(16):
        qg = qt * 512
        i = np.arange(qg, qg + 512)
        if kind == "window":
            lo = max(0, (qg - (WINDOW - 1)) // 128)
            hi = (qg + 511) // 128
            cand = list(range(lo, hi + 1))
            if lo > 0:
                cand = [0] + cand
        else:
            cand = list(range(0, (qg + 511) // 128 + 1))
        kts = []
        for kt in cand:
            j = np.arange(kt * 128, kt * 128 + 128)
            vis = vis_fn(i[None, :], j[:, None])
            if not vis.any():
                continue
            kts.append(kt)
            if vis.all():
                tilemap[(qt, kt)] = None
            else:
                m = np.where(vis, np.float32(0.0), np.float32(NEG)).astype(BF)
                key = m.tobytes()
                if key not in phash:
                    phash[key] = len(patterns)
                    patterns.append(m)
                tilemap[(qt, kt)] = phash[key]
        klists[qt] = kts
    return klists, tilemap, np.stack(patterns)


def emit_attention(prog, psum, env, *, kind, scale, q_sb, k_dram, v_dram,
                   masks_sb, mask_tok, klists, tilemap, q_tok, kv_tok,
                   evac_o, k_run_ap=None, q_ap=None, nkv=None, block_align=None):
    nc = prog.nc
    if nkv is None:
        nkv = k_dram.shape[0]
    if q_ap is None:
        q_ap = lambda h, qg: q_sb[:, h, qg:qg + 512]
    if k_run_ap is None:
        k_run_ap = lambda hk, a, b: k_dram[hk, a:b].rearrange("n p k -> p n k")
    kbuf, vbuf = env["kbuf"], env["vbuf"]
    kwar = env.setdefault("kv_war_" + kind, [None, None])
    ppool = env["ppool"]
    lrow = env["lrow"]
    bcast = env["bcast"]
    bc_i = env.setdefault("bc_i_" + kind, 0)
    for qt in range(16):
        qg = qt * 512
        for h in range(2):
            hk = h % nkv
            kts = klists[qt]
            blocks = [kts[i:i + 8] for i in range(0, len(kts), 8)]
            load_toks = {}

            def issue(bi, war_tok):
                if bi >= len(blocks):
                    return
                blk = blocks[bi]
                slot = bi % 2
                runs = []
                for kt in blk:
                    if (runs and runs[-1][1] == kt
                            and not (block_align and kt % block_align == 0)):
                        runs[-1] = (runs[-1][0], kt + 1)
                    else:
                        runs.append((kt, kt + 1))
                g = prog.dma_group()
                off = 0
                for (a, b) in runs:
                    n = b - a
                    g.dma("sync",
                          lambda e, o=kbuf[:, slot, off:off + n, :],
                                 i=k_run_ap(hk, a, b):
                              e.dma_start(o, i),
                          [war_tok, kv_tok])
                    g.dma("scalar",
                          lambda e, o=vbuf[:, slot, off:off + n, :],
                                 i=v_dram[hk, a:b]:
                              e.dma_start(o, i.rearrange("n p k -> p n k")),
                          [war_tok, kv_tok])
                    off += n
                load_toks[bi] = g.seal()

            issue(0, kwar[0])
            issue(1, kwar[1])

            psL, warL, biL = psum.acquire()
            psO, warO, biO = psum.acquire()
            first = True
            n_total = len(kts)
            n_seen = 0
            l_tok = o_tok = None
            for bi, blk in enumerate(blocks):
                slot = bi % 2
                for n_in_b, kt in enumerate(blk):
                    n_seen += 1
                    last = (n_seen == n_total)
                    psS, warS, biS = psum.acquire()
                    mm = prog.op(
                        "tensor",
                        lambda e, o=psS[:, :], l=kbuf[:, slot, n_in_b, :],
                               r=q_ap(h, qg):
                            e.matmul(o, l, r, start=True, stop=True),
                        [load_toks[bi], q_tok, warS])
                    pat = tilemap[(qt, kt)]
                    if pat is not None:
                        mm = prog.op(
                            "vector",
                            lambda e, o=psS[:, :], a=psS[:, :],
                                   b=masks_sb[:, pat, :]:
                                e.tensor_add(o, a, b),
                            [mm, mask_tok])
                    ptile, pwar, pi = ppool.acquire()
                    e_tok = prog.op(
                        "scalar",
                        lambda e, o=ptile[:, :], i=psS[:, :], s=scale:
                            e.activation(o, i, AF.Exp, scale=s),
                        [mm, pwar])
                    psum.release(biS, e_tok)
                    l_tok = prog.op(
                        "tensor",
                        lambda e, o=psL[0:1, :], l=env["ones_bf"][:, 0:1],
                               r=ptile[:, :], s=first, p=last:
                            e.matmul(o, l, r, start=s, stop=p),
                        [e_tok, env["ones_tok"]] + ([warL] if first else []),
                        inc=False)
                    o_tok = prog.op(
                        "tensor",
                        lambda e, o=psO[:, :], l=vbuf[:, slot, n_in_b, :],
                               r=ptile[:, :], s=first, p=last:
                            e.matmul(o, l, r, start=s, stop=p),
                        [load_toks[bi]] + ([warO] if first else []))
                    ppool.release(pi, o_tok)
                    first = False
                kwar[slot] = o_tok
                issue(bi + 2, o_tok)
            r_tok = prog.op(
                "vector",
                lambda e, o=lrow[0:1, :], i=psL[0:1, :]:
                    e.reciprocal(o, i),
                [o_tok, l_tok])
            psum.release(biL, r_tok)
            psB, warB, biB = psum.acquire()
            b_mm = prog.op(
                "tensor",
                lambda e, o=psB[:, :], l=env["ones_row"][0:1, :],
                       r=lrow[0:1, :]:
                    e.matmul(o, l, r, start=True, stop=True),
                [r_tok, env["ones_tok"], warB])
            bslot = env["bc_i_" + kind] % 2
            env["bc_i_" + kind] += 1
            bc_sb = bcast[bslot]
            bc_tok = prog.op(
                "vector",
                lambda e, o=bc_sb[:, :], i=psB[:, :]:
                    e.tensor_copy(o, i),
                [b_mm, env.get("bc_war_%s_%d" % (kind, bslot))])
            psum.release(biB, bc_tok)
            c_tok = evac_o(qt, h, psO, bc_sb, [bc_tok, o_tok])
            env["bc_war_%s_%d" % (kind, bslot)] = c_tok
            psum.release(biO, c_tok)


def build_launch2():
    nc = bass.Bass("TRN2", target_bir_lowering=False)
    q0h = nc.dram_tensor("q0h", [2, PART, T_FULL], BF16, kind="ExternalInput").ap()
    k0h = nc.dram_tensor("k0h", [1, 64, PART, PART], BF16, kind="ExternalInput").ap()
    v0h = nc.dram_tensor("v0h", [1, 64, PART, PART], BF16, kind="ExternalInput").ap()
    q1h = nc.dram_tensor("q1h", [2, PART, T_FULL], BF16, kind="ExternalInput").ap()
    k1h = nc.dram_tensor("k1h", [2, 64, PART, PART], BF16, kind="ExternalInput").ap()
    v1h = nc.dram_tensor("v1h", [2, 64, PART, PART], BF16, kind="ExternalInput").ap()
    gsl = nc.dram_tensor("gsl", [2, PART, 16], BF16, kind="ExternalInput").ap()
    kw, tw, pw = mask_patterns("window")
    kc_, tc_, pc_ = mask_patterns("causal")
    masks_w = nc.dram_tensor("masks_w", [PART, len(pw), 512], BF16,
                             kind="ExternalInput").ap()
    masks_c = nc.dram_tensor("masks_c", [PART, len(pc_), 512], BF16,
                             kind="ExternalInput").ap()
    o_o0 = nc.dram_tensor("o_o0", [2, PART, T_FULL], BF16, kind="ExternalOutput").ap()
    o_mod = nc.dram_tensor("o_mod", [16, T_FULL], F32, kind="ExternalOutput").ap()
    cc_in = nc.dram_tensor("cc_in", [16, T_FULL], F32)
    cc_out = nc.dram_tensor("cc_out", [16, T_FULL], F32)

    stack = ExitStack()
    with stack:
        prog = Prog(nc, stack)
        psum = PsumPool(prog, stack, 8)
        sb = lambda n_, sh, dt: stack.enter_context(nc.sbuf_tensor(n_, sh, dt))

        q0_sb = sb("q0_sb", [PART, 2, T_FULL], BF16)
        q1_sb = sb("q1_sb", [PART, 2, T_FULL], BF16)
        g_sb = sb("g_sb", [PART, 2, 16], BF16)
        mw_sb = sb("mw_sb", [PART, len(pw), 512], BF16)
        mc_sb = sb("mc_sb", [PART, len(pc_), 512], BF16)
        modp = sb("modp", [16, T_FULL], F32)

        env = {
            "kbuf": sb("kbuf", [PART, 2, 8, PART], BF16),
            "vbuf": sb("vbuf", [PART, 2, 8, PART], BF16),
            "ppool": StagePool(prog, stack, "pp", 4, (PART, 512), BF16),
            "lrow": sb("lrow", [1, 512], F32),
            "bcast": [sb("bc0", [PART, 512], F32), sb("bc1", [PART, 512], F32)],
            "ones_bf": sb("ones_bf", [PART, 1], BF16),
            "ones_row": sb("ones_row2", [1, PART], F32),
        }
        t1 = prog.op("vector", lambda e: e.memset(env["ones_bf"][:, :], 1.0))
        t2 = prog.op("vector", lambda e: e.memset(env["ones_row"][:, :], 1.0))
        env["ones_tok"] = t2

        gl = prog.dma_group()
        gl.dma("sync", lambda e: e.dma_start(
            q0_sb[:, :, :], q0h.rearrange("h p t -> p h t")))
        gl.dma("sync", lambda e: e.dma_start(
            q1_sb[:, :, :], q1h.rearrange("h p t -> p h t")))
        gl.dma("sync", lambda e: e.dma_start(
            g_sb[:, :, :], gsl.rearrange("h p s -> p h s")))
        gl.dma("sync", lambda e: e.dma_start(mw_sb[:, :, :], masks_w[:, :, :]))
        gl.dma("sync", lambda e: e.dma_start(mc_sb[:, :, :], masks_c[:, :, :]))
        t_in = gl.seal()

        out_stage = StagePool(prog, stack, "ost", 4, (PART, 512), BF16)
        scale = 1.0 / np.sqrt(128.0)

        def evac_o0(qt, h, ps_o, bc_sb, deps):
            tile, war, si = out_stage.acquire()
            tt = prog.op("vector",
                         lambda e, o=tile[:, :], a=ps_o[:, :], b=bc_sb[:, :]:
                             e.tensor_mul(o, a, b),
                         deps + [war])
            dtok = prog.dma1("gpsimd",
                             lambda e, o=o_o0[h, :, qt * 512:(qt + 1) * 512],
                                    i=tile[:, :]:
                                 e.dma_start(o, i), [tt])
            out_stage.release(si, dtok)
            return tt

        emit_attention(prog, psum, env, kind="window", scale=scale,
                       q_sb=q0_sb, k_dram=k0h, v_dram=v0h,
                       masks_sb=mw_sb, mask_tok=t_in, klists=kw, tilemap=tw,
                       q_tok=t_in, kv_tok=None, evac_o=evac_o0)

        prog.barrier()

        o1_pool = StagePool(prog, stack, "o1p", 2, (PART, 512), BF16)
        mod_banks = {}

        def evac_o1(qt, h, ps_o, bc_sb, deps):
            tile, war, si = o1_pool.acquire()
            tt = prog.op("vector",
                         lambda e, o=tile[:, :], a=ps_o[:, :], b=bc_sb[:, :]:
                             e.tensor_mul(o, a, b),
                         deps + [war])
            if h == 0:
                mod_banks[qt] = psum.acquire()
            psM, warM, biM = mod_banks[qt]
            g_mm = prog.op(
                "tensor",
                lambda e, o=psM[0:16, :], l=g_sb[:, h, :], r=tile[:, :],
                       s=(h == 0):
                    e.matmul(o, l, r, start=s, stop=(h == 1)),
                [tt, t_in] + ([warM] if h == 0 else []))
            o1_pool.release(si, g_mm)
            if h == 1:
                cp = prog.op("vector",
                             lambda e, o=modp[:, qt * 512:(qt + 1) * 512],
                                    i=psM[0:16, :]:
                                 e.tensor_copy(o, i), [g_mm])
                psum.release(biM, cp)
            return tt

        emit_attention(prog, psum, env, kind="causal", scale=scale,
                       q_sb=q1_sb, k_dram=k1h, v_dram=v1h,
                       masks_sb=mc_sb, mask_tok=t_in, klists=kc_, tilemap=tc_,
                       q_tok=t_in, kv_tok=None, evac_o=evac_o1)

        prog.barrier()
        t_st = prog.dma1("gpsimd",
                         lambda e: e.dma_start(cc_in.ap(), modp[:, :]))
        t_cc = prog.op("gpsimd",
                       lambda e: e.collective_compute(
                           "AllReduce", ALU.add,
                           replica_groups=[list(range(NCORES))],
                           ins=[cc_in.ap().opt()], outs=[cc_out.ap().opt()]),
                       [t_st])
        t_out = prog.dma1("gpsimd",
                          lambda e: e.dma_start(o_mod, cc_out.ap()), [t_cc])
        prog.barrier()
        block = stack.enter_context(nc.Block())
        prog.install(block)
    return nc



def build_launch3():
    nc = bass.Bass("TRN2", target_bir_lowering=False)
    xT = nc.dram_tensor("xT", [D, TC], F32, kind="ExternalInput").ap()
    xnT = nc.dram_tensor("xnT", [D, TC], F32, kind="ExternalInput").ap()
    o0tok = nc.dram_tensor("o0tok", [D, TC], BF16, kind="ExternalInput").ap()
    modT = nc.dram_tensor("modT", [16, TC], F32, kind="ExternalInput").ap()
    hT = nc.dram_tensor("hT", [D, TC], BF16, kind="ExternalInput").ap()
    wo0t = nc.dram_tensor("wo0t", [D, D], F32, kind="ExternalInput").ap()
    w_int = nc.dram_tensor("w_int", [D, 8192], F32, kind="ExternalInput").ap()
    w_upt = nc.dram_tensor("w_upt", [D, 8192], F32, kind="ExternalInput").ap()
    w_downt = nc.dram_tensor("w_downt", [8192, D], F32, kind="ExternalInput").ap()
    a_n = nc.dram_tensor("a_n", [D, 16], F32, kind="ExternalInput").ap()
    b_n = nc.dram_tensor("b_n", [16, 8192], F32, kind="ExternalInput").ap()
    l1w = nc.dram_tensor("l1w", [PART, 16], F32, kind="ExternalInput").ap()
    l1b = nc.dram_tensor("l1b", [PART, 16], F32, kind="ExternalInput").ap()

    o_x1 = nc.dram_tensor("o_x1", [D, TC], F32, kind="ExternalOutput").ap()
    o_xn2 = nc.dram_tensor("o_xn2", [D, TC], F32, kind="ExternalOutput").ap()
    o_xn2b = nc.dram_tensor("o_xn2b", [D, TC], BF16, kind="ExternalOutput").ap()
    o_u = nc.dram_tensor("o_u", [D, TC], BF16, kind="ExternalOutput").ap()

    prod_d = nc.dram_tensor("prod_d", [8192, TC], BF16)

    stack = ExitStack()
    with stack:
        prog = Prog(nc, stack)
        psum = PsumPool(prog, stack, 8)
        misc = Misc(prog, stack, TC)
        sb = lambda n_, sh, dt: stack.enter_context(nc.sbuf_tensor(n_, sh, dt))

        acc = sb("acc", [PART, 16, TC], F32)
        actb = sb("actb", [PART, 16, TC], BF16)
        xtile = [sb("xt0", [PART, 512], F32), sb("xt1", [PART, 512], F32)]
        wbufA = sb("wbufA", [PART, 16, 2, 128], BF16)
        wbufB = sb("wbufB", [PART, 16, 2, 128], BF16)
        wdC = sb("wdC", [PART, 8, 1024], BF16)
        prodblk = sb("prodblk", [PART, 8, TC], BF16)
        mrow = sb("mrow", [16, TC], F32)
        hA = sb("hA", [16, TC], F32)
        mm_sb = sb("mm_sb", [16, TC], BF16)
        b_sb = sb("b_sb", [16, 4096], BF16)
        gtmp = [sb("gt0", [PART, 512], F32), sb("gt1", [PART, 512], F32)]
        ptile = StagePool(prog, stack, "ptile", 2, (PART, 512), BF16)
        l1w_sb = sb("l1w_sb", [PART, 16], F32)
        l1b_sb = sb("l1b_sb", [PART, 16], F32)

        xn_view = xnT.rearrange("(c p) t -> p c t", p=PART)
        x_view = xT.rearrange("(c p) t -> p c t", p=PART)
        o0_view = o0tok.rearrange("(c p) t -> p c t", p=PART)
        h_view = hT.rearrange("(c p) t -> p c t", p=PART)

        gl = prog.dma_group()
        gl.dma("sync", lambda e: e.dma_start(acc[:, :, :], xn_view))
        gl.dma("sync", lambda e: e.dma_start(actb[:, :, :], o0_view))
        gl.dma("sync", lambda e: e.dma_start(mrow[:, :], modT))
        gl.dma("sync", lambda e: e.dma_start(l1w_sb[:, :], l1w))
        gl.dma("sync", lambda e: e.dma_start(l1b_sb[:, :], l1b))
        t_in0 = gl.seal()
        t_bsb = prog.dma1("gpsimd", lambda e: e.dma_start(b_sb[:, :], b_n[:, 0:4096]))
        t_in = t_bsb if False else t_in0
        _ = t_bsb

        ws0 = WStreamer(prog, wbufA, wo0t, D, D, mc=128, init_war=None)
        xw = [None, None]

        def evac_comb(mi, ti, ps, mm_tok):
            sl = (mi * 2 + ti) % 2
            xt_tok = prog.dma1("scalar",
                               lambda e, o=xtile[sl][:, :],
                                      i=x_view[:, mi, ti * 512:(ti + 1) * 512]:
                                   e.dma_start(o, i), [xw[sl]])
            t1 = prog.op("vector",
                         lambda e, o=acc[:, mi, ti * 512:(ti + 1) * 512], a=ps,
                                b=acc[:, mi, ti * 512:(ti + 1) * 512]:
                             e.tensor_add(o, a, b), [mm_tok, t_in])
            t2 = prog.op("vector",
                         lambda e, o=acc[:, mi, ti * 512:(ti + 1) * 512],
                                a=xtile[sl][:, :],
                                b=acc[:, mi, ti * 512:(ti + 1) * 512]:
                             e.tensor_add(o, a, b), [t1, xt_tok])
            xw[sl] = t2
            return t1

        gemm(prog, psum, M=D, K=D, T=TC,
             lhsT=ws0.lhsT,
             rhs=lambda ki, ti: actb[:, ki, ti * 512:(ti + 1) * 512],
             lhs_dep=ws0.dep, rhs_dep=lambda ki, ti: t_in,
             evac=evac_comb, m_done=ws0.m_done)
        prog.barrier()

        t_h = prog.dma1("sync", lambda e: e.dma_start(actb[:, :, :], h_view))
        a_sb, t_a = load_weight_bf16(prog, stack, "a_sb3", a_n, D, 16)

        def hA_evac(mi, ti, ps, mm_tok):
            return prog.op("vector",
                           lambda e, o=hA[0:16, ti * 512:(ti + 1) * 512], i=ps:
                               e.tensor_copy(o, i), [mm_tok])

        gemm(prog, psum, M=16, K=D, T=TC,
             lhsT=lambda mi, ki: a_sb[:, ki, :],
             rhs=lambda ki, ti: actb[:, ki, ti * 512:(ti + 1) * 512],
             lhs_dep=lambda mi, ki: t_a, rhs_dep=lambda ki, ti: t_h,
             evac=hA_evac)
        t_tanh = prog.op("scalar",
                         lambda e: e.activation(mrow[0:16, :], mrow[0:16, :],
                                                AF.Tanh), [t_in])
        t_modv = prog.op("vector",
                         lambda e: e.tensor_scalar_mul(mrow[0:16, :],
                                                       mrow[0:16, :], 0.1),
                         [t_tanh])
        t_mm = prog.op("vector",
                       lambda e: e.tensor_mul(mm_sb[0:16, :], mrow[0:16, :],
                                              hA[0:16, :]),
                       [t_modv, prog.token("vector")])

        wsi = WStreamer(prog, wbufA, w_int, D, 8192, mc=128, init_war=prog.token("tensor"))
        wsu = WStreamer(prog, wbufB, w_upt, D, 8192, mc=128, init_war=None)
        prod_view = prod_d.rearrange("(c p) t -> p c t", p=PART)
        gw = [None, None]
        t_bh = t_bsb
        for mi in range(64):
            if mi == 32:
                t_bh = prog.dma1("gpsimd",
                                 lambda e: e.dma_start(b_sb[:, :],
                                                       b_n[:, 4096:8192]),
                                 [prog.token("tensor")])
            for ti in range(2):
                t0 = ti * 512
                psG, warG, biG = psum.acquire()
                psU, warU, biU = psum.acquire()
                mmG = mmU = None
                for ki in range(16):
                    mmG = prog.op(
                        "tensor",
                        lambda e, o=psG[:, :], l=wsi.lhsT(mi, ki),
                               r=actb[:, ki, t0:t0 + 512], s=(ki == 0):
                            e.matmul(o, l, r, start=s, stop=False),
                        [wsi.dep(mi, ki), t_h] + ([warG] if ki == 0 else []),
                        inc=False)
                    mmU = prog.op(
                        "tensor",
                        lambda e, o=psU[:, :], l=wsu.lhsT(mi, ki),
                               r=actb[:, ki, t0:t0 + 512], s=(ki == 0),
                               p=(ki == 15):
                            e.matmul(o, l, r, start=s, stop=p),
                        [wsu.dep(mi, ki)] + ([warU] if ki == 0 else []),
                        inc=(ki == 15))
                moff = (mi % 32) * 128
                mmG = prog.op(
                    "tensor",
                    lambda e, o=psG[:, :], l=b_sb[0:16, moff:moff + 128],
                           r=mm_sb[0:16, t0:t0 + 512]:
                        e.matmul(o, l, r, start=False, stop=True),
                    [t_mm, t_bh])
                sl = ti
                t_si = prog.op(
                    "scalar",
                    lambda e, o=gtmp[sl][:, :], i=psG[:, :]:
                        e.activation(o, i, AF.Silu), [mmG, gw[sl]])
                psum.release(biG, t_si)
                pt, pwar, pi = ptile.acquire()
                t_pr = prog.op(
                    "vector",
                    lambda e, o=pt[:, :], a=psU[:, :], b=gtmp[sl][:, :]:
                        e.tensor_mul(o, a, b), [t_si, mmU, pwar])
                gw[sl] = t_pr
                psum.release(biU, t_pr)
                dtok = prog.dma1("gpsimd",
                                 lambda e, o=prod_view[:, mi, t0:t0 + 512],
                                        i=pt[:, :]:
                                     e.dma_start(o, i), [t_pr])
                ptile.release(pi, dtok)
            wsi.m_done(mi, prog.token("tensor"))
            wsu.m_done(mi, prog.token("tensor"))
        prog.barrier()

        wd_view = w_downt.rearrange("(kc p) m -> p kc m", p=PART)
        for kb in range(8):
            t_pb = prog.dma1("sync",
                             lambda e, o=prodblk[:, :, :],
                                    i=prod_view[:, kb * 8:(kb + 1) * 8, :]:
                                 e.dma_start(o, i),
                             [prog.token("tensor") if kb else None])
            for mh in range(2):
                t_wd = prog.dma1(
                    "gpsimd",
                    lambda e, o=wdC[:, :, :],
                           i=wd_view[:, kb * 8:(kb + 1) * 8,
                                     mh * 1024:(mh + 1) * 1024]:
                        e.dma_start(o, i),
                    [prog.token("tensor")])
                for mj in range(8):
                    mi = mh * 8 + mj
                    for ti in range(2):
                        t0 = ti * 512
                        bank, war, bi = psum.acquire()
                        mm = None
                        for kj in range(8):
                            mm = prog.op(
                                "tensor",
                                lambda e, o=bank[:, :],
                                       l=wdC[:, kj, mj * 128:(mj + 1) * 128],
                                       r=prodblk[:, kj, t0:t0 + 512],
                                       s=(kj == 0), p=(kj == 7):
                                    e.matmul(o, l, r, start=s, stop=p),
                                [t_wd, t_pb] + ([war] if kj == 0 else []),
                                inc=(kj == 7))
                        tadd = prog.op(
                            "vector",
                            lambda e, o=acc[:, mi, t0:t0 + 512], a=bank[:, :],
                                   b=acc[:, mi, t0:t0 + 512]:
                                e.tensor_add(o, a, b), [mm])
                        psum.release(bi, tadd)
        prog.barrier()

        x1_view = o_x1.rearrange("(c p) t -> p c t", p=PART)
        g1 = prog.dma_group()
        for ch in range(16):
            g1.dma("sync", lambda e, ch=ch: e.dma_start(
                x1_view[:, ch, :], acc[:, ch, :]), [prog.token("vector")])
        t_x1o = g1.seal()
        t_xn2 = layernorm(prog, psum, misc, x=acc, nch=16, T=TC,
                          out_f32=acc, out_bf16=actb, eps=1e-5,
                          use_mean=True, dep=t_x1o)
        xn2_view = o_xn2.rearrange("(c p) t -> p c t", p=PART)
        xn2b_view = o_xn2b.rearrange("(c p) t -> p c t", p=PART)
        g2 = prog.dma_group()
        for ch in range(16):
            g2.dma("sync", lambda e, ch=ch: e.dma_start(
                xn2_view[:, ch, :], acc[:, ch, :]), [t_xn2])
            g2.dma("scalar", lambda e, ch=ch: e.dma_start(
                xn2b_view[:, ch, :], actb[:, ch, :]), [t_xn2])
        t_xn2o = g2.seal()
        t_u = layernorm(prog, psum, misc, x=acc, nch=16, T=TC,
                        out_bf16=actb, gamma=l1w_sb, beta=l1b_sb, eps=1e-5,
                        use_mean=True, dep=t_xn2o)
        u_view = o_u.rearrange("(c p) t -> p c t", p=PART)
        g3 = prog.dma_group()
        for ch in range(16):
            g3.dma("sync", lambda e, ch=ch: e.dma_start(
                u_view[:, ch, :], actb[:, ch, :]), [t_u])
        g3.seal()
        prog.barrier()
        block = stack.enter_context(nc.Block())
        prog.install(block)
    return nc



def build_launch4(tfull=T_FULL):
    nc = bass.Bass("TRN2", target_bir_lowering=False)
    uT = nc.dram_tensor("uT", [D, tfull], BF16, kind="ExternalInput").ap()
    ipT = nc.dram_tensor("ipT", [D, 1024], F32, kind="ExternalInput").ap()
    convw = nc.dram_tensor("convw", [PART, 4, 4], F32, kind="ExternalInput").ap()
    convb = nc.dram_tensor("convb", [PART, 4], F32, kind="ExternalInput").ap()
    xpT = nc.dram_tensor("xpT", [512, 160], F32, kind="ExternalInput").ap()
    dtwT = nc.dram_tensor("dtwT", [PART, 512], F32, kind="ExternalInput").ap()
    dtb = nc.dram_tensor("dtb", [PART, 4], F32, kind="ExternalInput").ap()

    o_xc = nc.dram_tensor("o_xc", [512, tfull], F32, kind="ExternalOutput").ap()
    o_z = nc.dram_tensor("o_z", [512, tfull], F32, kind="ExternalOutput").ap()
    o_dt = nc.dram_tensor("o_dt", [512, tfull], F32, kind="ExternalOutput").ap()
    o_bc = nc.dram_tensor("o_bc", [32, tfull], F32, kind="ExternalOutput").ap()

    cc_in = nc.dram_tensor("cc_in4", [160, tfull], F32)
    cc_out = nc.dram_tensor("cc_out4", [160, tfull], F32)

    stack = ExitStack()
    with stack:
        prog = Prog(nc, stack)
        psum = PsumPool(prog, stack, 8)
        sb = lambda n_, sh, dt: stack.enter_context(nc.sbuf_tensor(n_, sh, dt))

        ip_sb, t_ip = load_weight_bf16(prog, stack, "ip_sb", ipT, D, 1024)
        xp_sb = sb("xp_sb", [PART, 4, 160], BF16)
        dtw_sb = sb("dtw_sb", [PART, 512], BF16)
        cw_sb = sb("cw_sb", [PART, 4, 4], F32)
        cb_sb = sb("cb_sb", [PART, 4], F32)
        dtb_sb = sb("dtb_sb", [PART, 4], F32)
        ubuf = [sb("ub0", [PART, 16, 512], BF16), sb("ub1", [PART, 16, 512], BF16)]
        xm_sb = sb("xm_sb", [PART, 4, 515], F32)
        halo = sb("halo", [PART, 4, 3], F32)
        xct = sb("xct", [PART, 4, 512], F32)
        xcb = sb("xcb", [PART, 4, 512], BF16)
        ztile = StagePool(prog, stack, "zt", 2, (PART, 512), F32)
        xc_stage = StagePool(prog, stack, "xcs", 3, (PART, 4, 512), F32)
        xdblA = sb("xdblA", [PART, tfull], F32)
        xdblB = sb("xdblB", [32, tfull], F32)
        dtr_sb = sb("dtr_sb", [PART, tfull], BF16)
        spt = sb("spt", [PART, 2, 512], F32)
        dt_stage = StagePool(prog, stack, "dts", 2, (PART, 512), F32)

        gsw = prog.dma_group(sw=True)
        gsw.dma("gpsimd", lambda e: e.dma_start(
            xp_sb[:, :, :], xpT.rearrange("(c p) s -> p c s", p=PART)))
        gsw.dma("gpsimd", lambda e: e.dma_start(dtw_sb[:, :], dtwT))
        t_insw = gsw.seal()
        gl = prog.dma_group()
        gl.dma("sync", lambda e: e.dma_start(cw_sb[:, :, :], convw))
        gl.dma("sync", lambda e: e.dma_start(cb_sb[:, :], convb))
        gl.dma("sync", lambda e: e.dma_start(dtb_sb[:, :], dtb))
        t_inhw = gl.seal()
        t_in = t_inhw
        t_z0 = prog.op("vector", lambda e: e.memset(halo[:, :, :], 0.0))

        u_view = uT.rearrange("(c p) t -> p c t", p=PART)
        uw = [None, None]
        t_halo = t_z0
        xm_war = [None] * 4
        xct_war = [None] * 4
        spt_war = [None, None]
        for tt in range(tfull // 512):
            t0 = tt * 512
            sl = tt % 2
            t_u = prog.dma1("sync",
                            lambda e, o=ubuf[sl][:, :, :],
                                   i=u_view[:, :, t0:t0 + 512]:
                                e.dma_start(o, i), [uw[sl]])
            xm_toks = []
            for g in range(8):
                bank, war, bi = psum.acquire()
                mm = None
                for ki in range(16):
                    mm = prog.op(
                        "tensor",
                        lambda e, o=bank[:, :], l=ip_sb[:, ki, g * 128:(g + 1) * 128],
                               r=ubuf[sl][:, ki, :], s=(ki == 0), p=(ki == 15):
                            e.matmul(o, l, r, start=s, stop=p),
                        [t_ip, t_u] + ([war] if ki == 0 else []),
                        inc=(ki == 15))
                if g < 4:
                    tk = prog.op("vector",
                                 lambda e, o=xm_sb[:, g, 3:515], i=bank[:, :]:
                                     e.tensor_copy(o, i),
                                 [mm, t_halo, xm_war[g]])
                    xm_toks.append(tk)
                else:
                    zt_, zwar, zi = ztile.acquire()
                    tk = prog.op("vector",
                                 lambda e, o=zt_[:, :], i=bank[:, :]:
                                     e.tensor_copy(o, i), [mm, zwar])
                    dtok = prog.dma1("gpsimd",
                                     lambda e, o=o_z[(g - 4) * 128:(g - 3) * 128,
                                                     t0:t0 + 512], i=zt_[:, :]:
                                         e.dma_start(o, i), [tk])
                    ztile.release(zi, dtok)
                psum.release(bi, tk)
            t_hcopy = prog.op("vector",
                              lambda e: e.tensor_copy(xm_sb[:, :, 0:3],
                                                      halo[:, :, :]),
                              xm_toks + [t_z0, t_halo])
            xcs_tile, xcs_war, xcs_i = xc_stage.acquire()
            conv_toks = []
            for db in range(4):
                tk = prog.op("vector",
                             lambda e, db=db:
                                 e.tensor_scalar(xct[:, db, :], xm_sb[:, db, 0:512],
                                                 cw_sb[:, db, 0:1], None, ALU.mult),
                             [xm_toks[db], t_hcopy, t_in, xct_war[db]])
                for kk in range(1, 4):
                    tk = prog.op("vector",
                                 lambda e, db=db, kk=kk:
                                     e.scalar_tensor_tensor(
                                         xct[:, db, :], xm_sb[:, db, kk:kk + 512],
                                         cw_sb[:, db, kk:kk + 1], xct[:, db, :],
                                         ALU.mult, ALU.add),
                                 [tk])
                xm_war[db] = tk
                tk = prog.op("scalar",
                             lambda e, o=xcs_tile[:, db, :], i=xct[:, db, :],
                                    b=cb_sb[:, db:db + 1]:
                                 e.activation(o, i, AF.Silu, bias=b),
                             [tk, xcs_war])
                xct_war[db] = tk
                tkb = prog.op("vector",
                              lambda e, o=xcb[:, db, :], i=xcs_tile[:, db, :]:
                                  e.tensor_copy(o, i),
                              [tk])
                conv_toks.append(tkb)
            t_halo = prog.op("vector",
                             lambda e: e.tensor_copy(halo[:, :, :],
                                                     xm_sb[:, :, 512:515]),
                             xm_toks + [t_hcopy])
            dtok = prog.dma1("sync",
                             lambda e, o=o_xc.rearrange("(c p) t -> p c t", p=PART)
                                    [:, :, t0:t0 + 512], i=xcs_tile[:, :, :]:
                                 e.dma_start(o, i), conv_toks)
            xc_stage.release(xcs_i, dtok)
            for (r0, rw) in ((0, PART), (PART, 32)):
                bank, war, bi = psum.acquire()
                mm = None
                for db in range(4):
                    mm = prog.op(
                        "tensor",
                        lambda e, o=bank[0:rw, :], l=xp_sb[:, db, r0:r0 + rw],
                               r=xcb[:, db, :], s=(db == 0), p=(db == 3):
                            e.matmul(o, l, r, start=s, stop=p),
                        [t_insw, conv_toks[db]] + ([war] if db == 0 else []),
                        inc=(db == 3))
                dst = xdblA[:, t0:t0 + 512] if r0 == 0 else xdblB[0:32, t0:t0 + 512]
                tk = prog.op("vector",
                             lambda e, o=dst, i=bank[0:rw, :]:
                                 e.tensor_copy(o, i), [mm])
                psum.release(bi, tk)
            uw[sl] = prog.token("tensor")

        prog.barrier()
        ga = prog.dma_group(sw=True)
        ga.dma("gpsimd", lambda e: e.dma_start(cc_in.ap()[0:PART, :], xdblA[:, :]))
        ga.dma("gpsimd", lambda e: e.dma_start(cc_in.ap()[PART:160, :], xdblB[0:32, :]))
        t_st = ga.seal()
        t_cc = prog.op("gpsimd",
                       lambda e: e.collective_compute(
                           "AllReduce", ALU.add,
                           replica_groups=[list(range(NCORES))],
                           ins=[cc_in.ap().opt()], outs=[cc_out.ap().opt()]),
                       [t_st])
        t_bc = prog.dma1("gpsimd",
                         lambda e: e.dma_start(o_bc, cc_out.ap()[PART:160, :]),
                         [t_cc])
        t_dtr = prog.dma1("gpsimd",
                          lambda e: e.dma_start(dtr_sb[:, :], cc_out.ap()[0:PART, :]),
                          [t_cc])
        for mi in range(4):
            for tt in range(tfull // 512):
                t0 = tt * 512
                bank, war, bi = psum.acquire()
                mm = prog.op(
                    "tensor",
                    lambda e, o=bank[:, :], l=dtw_sb[:, mi * 128:(mi + 1) * 128],
                           r=dtr_sb[:, t0:t0 + 512]:
                        e.matmul(o, l, r, start=True, stop=True),
                    [t_insw, t_dtr, war])
                sl = tt % 2
                t_v = prog.op("vector",
                              lambda e, o=spt[:, sl, :], i=bank[:, :], b=dtb_sb[:, mi:mi + 1]:
                                  e.tensor_scalar_add(o, i, b),
                              [mm, t_in, spt_war[sl]])
                psum.release(bi, t_v)
                dtile, dwar, di = dt_stage.acquire()
                t_ab = prog.op("scalar",
                               lambda e, o=dtile[:, :], i=spt[:, sl, :]:
                                   e.activation(o, i, AF.Abs),
                               [t_v, dwar])
                t_ex = prog.op("scalar",
                               lambda e, o=dtile[:, :], i=dtile[:, :]:
                                   e.activation(o, i, AF.Exp, scale=-1.0),
                               [t_ab])
                one_ap, one_tok = _misc_cc(prog, stack, 1.0)
                t_ln = prog.op("scalar",
                               lambda e, o=dtile[:, :], i=dtile[:, :], b=one_ap:
                                   e.activation(o, i, AF.Ln, bias=b),
                               [t_ex, one_tok])
                t_sp = prog.op("vector",
                               lambda e, o=dtile[:, :], a=spt[:, sl, :], b=dtile[:, :]:
                                   e.scalar_tensor_tensor(o, a, 0.0, b,
                                                          ALU.max, ALU.add),
                               [t_ln, t_v])
                spt_war[sl] = t_sp
                dtok = prog.dma1("sync",
                                 lambda e, o=o_dt[mi * 128:(mi + 1) * 128, t0:t0 + 512],
                                        i=dtile[:, :]:
                                     e.dma_start(o, i), [t_sp])
                dt_stage.release(di, dtok)
        prog.barrier()
        block = stack.enter_context(nc.Block())
        prog.install(block)
    return nc


_MISC_CC_CACHE = {}


def _misc_cc(prog, stack, value):
    key = (id(prog), value)
    if key not in _MISC_CC_CACHE:
        t = stack.enter_context(prog.nc.sbuf_tensor(f"cc_{len(_MISC_CC_CACHE)}",
                                                    [PART, 1], F32))
        tok = prog.op("vector", lambda e, t=t, v=value: e.memset(t[:, :], float(v)))
        _MISC_CC_CACHE[key] = (t, tok)
    t, tok = _MISC_CC_CACHE[key]
    return t[:, 0:1], tok



def build_launch5(tfull=T_FULL):
    nc = bass.Bass("TRN2", target_bir_lowering=False)
    xcS = nc.dram_tensor("xcS", [512, tfull], F32, kind="ExternalInput").ap()
    zS = nc.dram_tensor("zS", [512, tfull], F32, kind="ExternalInput").ap()
    dtS = nc.dram_tensor("dtS", [512, tfull], F32, kind="ExternalInput").ap()
    bcT = nc.dram_tensor("bcT", [32, tfull], F32, kind="ExternalInput").ap()
    alog = nc.dram_tensor("alog", [PART, 4, 16], F32, kind="ExternalInput").ap()
    dsk = nc.dram_tensor("dsk", [PART, 4], F32, kind="ExternalInput").ap()
    o_yg = nc.dram_tensor("o_yg", [512, tfull], BF16, kind="ExternalOutput").ap()

    LC = 1024
    ntc = tfull // LC

    stack = ExitStack()
    with stack:
        prog = Prog(nc, stack)
        psum = PsumPool(prog, stack, 8)
        sb = lambda n_, sh, dt: stack.enter_context(nc.sbuf_tensor(n_, sh, dt))

        an = sb("an", [PART, 4, 16], F32)
        dskb = sb("dskb", [PART, 4], F32)
        dtc = sb("dtc", [PART, 4, LC], F32)
        xcc = sb("xcc", [PART, 4, LC], F32)
        uc = sb("uc", [PART, 4, LC], F32)
        zc = sb("zc", [PART, 4, LC], F32)
        yacc = sb("yacc", [PART, 4, LC], F32)
        ygb = sb("ygb", [PART, 4, LC], BF16)
        dAt = [sb("dA0", [PART, LC], F32), sb("dA1", [PART, LC], F32)]
        bt = [sb("bt0", [PART, LC], F32), sb("bt1", [PART, LC], F32)]
        ht = [sb("ht0", [PART, LC], F32), sb("ht1", [PART, LC], F32)]
        tmp = [sb("tp0", [PART, LC], F32), sb("tp1", [PART, LC], F32)]
        Bb = sb("Bb", [PART, LC], F32)
        Cb = sb("Cb", [PART, LC], F32)
        brow = [sb("br0", [1, LC], F32), sb("br1", [1, LC], F32)]
        crow = [sb("cr0", [1, LC], F32), sb("cr1", [1, LC], F32)]
        state = sb("state", [PART, 64], F32)
        ones_row = sb("ones_r5", [1, PART], F32)

        xc_view = xcS.rearrange("(c p) t -> p c t", p=PART)
        z_view = zS.rearrange("(c p) t -> p c t", p=PART)
        dt_view = dtS.rearrange("(c p) t -> p c t", p=PART)
        yg_view = o_yg.rearrange("(c p) t -> p c t", p=PART)

        t_or = prog.op("vector", lambda e: e.memset(ones_row[:, :], 1.0))
        t_state = prog.op("vector", lambda e: e.memset(state[:, :], 0.0))

        gl = prog.dma_group()
        gl.dma("sync", lambda e: e.dma_start(an[:, :, :], alog))
        gl.dma("sync", lambda e: e.dma_start(dskb[:, :], dsk))
        t_al = gl.seal()
        t_an = prog.op("scalar", lambda e: e.activation(
            an[:, :, :], an[:, :, :], AF.Exp), [t_al])
        t_an = prog.op("vector", lambda e: e.tensor_scalar_mul(
            an[:, :, :], an[:, :, :], -1.0), [t_an])

        war = {"dA": [None, None], "bt": [None, None], "ht": [None, None],
               "tmp": [None, None], "brow": [None, None], "crow": [None, None],
               "Bb": None, "Cb": None, "dtc": None, "xcc": None, "uc": None,
               "zc": None, "yacc": [None] * 4, "ygb": [None] * 4}
        state_tok = [t_state] * 64

        for tc in range(ntc):
            t0 = tc * LC
            g = prog.dma_group()
            g.dma("sync", lambda e, o=dtc[:, :, :], i=dt_view[:, :, t0:t0 + LC]:
                  e.dma_start(o, i), [war["dtc"]])
            g.dma("sync", lambda e, o=xcc[:, :, :], i=xc_view[:, :, t0:t0 + LC]:
                  e.dma_start(o, i), [war["xcc"]])
            t_ld = g.seal()
            t_u = prog.op("vector", lambda e: e.tensor_mul(
                uc[:, :, :], dtc[:, :, :], xcc[:, :, :]), [t_ld, war["uc"]])
            for s in range(16):
                sl = s % 2
                t_br = prog.dma1("scalar",
                                 lambda e, o=brow[sl][0:1, :], i=bcT[s:s + 1, t0:t0 + LC]:
                                     e.dma_start(o, i), [war["brow"][sl]])
                t_cr = prog.dma1("scalar",
                                 lambda e, o=crow[sl][0:1, :], i=bcT[16 + s:17 + s, t0:t0 + LC]:
                                     e.dma_start(o, i), [war["crow"][sl]])
                btoks, ctoks = [], []
                for half in range(2):
                    c0 = half * 512
                    bank, bwar, bi = psum.acquire()
                    mm = prog.op("tensor",
                                 lambda e, o=bank[:, :], l=ones_row[0:1, :],
                                        r=brow[sl][0:1, c0:c0 + 512]:
                                     e.matmul(o, l, r, start=True, stop=True),
                                 [t_br, t_or, bwar])
                    tk = prog.op("vector",
                                 lambda e, o=Bb[:, c0:c0 + 512], i=bank[:, :]:
                                     e.tensor_copy(o, i), [mm, war["Bb"]])
                    psum.release(bi, tk)
                    btoks.append(tk)
                    bank, bwar, bi = psum.acquire()
                    mm = prog.op("tensor",
                                 lambda e, o=bank[:, :], l=ones_row[0:1, :],
                                        r=crow[sl][0:1, c0:c0 + 512]:
                                     e.matmul(o, l, r, start=True, stop=True),
                                 [t_cr, t_or, bwar])
                    tk = prog.op("vector",
                                 lambda e, o=Cb[:, c0:c0 + 512], i=bank[:, :]:
                                     e.tensor_copy(o, i), [mm, war["Cb"]])
                    psum.release(bi, tk)
                    ctoks.append(tk)
                war["brow"][sl] = btoks[-1]
                war["crow"][sl] = ctoks[-1]
                b_done = btoks + ctoks
                for db in range(4):
                    wsl = (s * 4 + db) % 2
                    idx = db * 16 + s
                    t_dA = prog.op("scalar",
                                   lambda e, o=dAt[wsl][:, :], i=dtc[:, db, :],
                                          sc=an[:, db, s:s + 1]:
                                       e.activation(o, i, AF.Exp, scale=sc),
                                   [t_ld, t_an, war["dA"][wsl]])
                    t_b = prog.op("vector",
                                  lambda e, o=bt[wsl][:, :], a=uc[:, db, :], b=Bb[:, :]:
                                      e.tensor_mul(o, a, b),
                                  [t_u, war["bt"][wsl]] + b_done)
                    t_h = prog.op("vector",
                                  lambda e, o=ht[wsl][:, :], a=dAt[wsl][:, :],
                                         b=bt[wsl][:, :], ini=state[:, idx:idx + 1]:
                                      e.tensor_tensor_scan(o, a, b, ini,
                                                           ALU.mult, ALU.add),
                                  [t_dA, t_b, state_tok[idx], war["ht"][wsl]])
                    war["dA"][wsl] = t_h
                    war["bt"][wsl] = t_h
                    t_st = prog.op("vector",
                                   lambda e, o=state[:, idx:idx + 1],
                                          i=ht[wsl][:, LC - 1:LC]:
                                       e.tensor_copy(o, i), [t_h])
                    state_tok[idx] = t_st
                    if s == 0:
                        t_y = prog.op("vector",
                                      lambda e, o=yacc[:, db, :], a=ht[wsl][:, :],
                                             b=Cb[:, :]:
                                          e.tensor_mul(o, a, b),
                                      [t_h, war["yacc"][db]] + b_done)
                    else:
                        t_m = prog.op("vector",
                                      lambda e, o=tmp[wsl][:, :], a=ht[wsl][:, :],
                                             b=Cb[:, :]:
                                          e.tensor_mul(o, a, b),
                                      [t_h, war["tmp"][wsl]] + b_done)
                        t_y = prog.op("vector",
                                      lambda e, o=yacc[:, db, :], a=yacc[:, db, :],
                                             b=tmp[wsl][:, :]:
                                          e.tensor_add(o, a, b),
                                      [t_m, war["yacc"][db]])
                        war["tmp"][wsl] = t_y
                    war["yacc"][db] = t_y
                    war["ht"][wsl] = t_y
                war["Bb"] = prog.token("vector")
                war["Cb"] = war["Bb"]
            t_z = prog.dma1("sync",
                            lambda e, o=zc[:, :, :], i=z_view[:, :, t0:t0 + LC]:
                                e.dma_start(o, i), [war["zc"]])
            out_toks = []
            for db in range(4):
                t_yd = prog.op("vector",
                               lambda e, o=yacc[:, db, :], a=xcc[:, db, :],
                                      sc=dskb[:, db:db + 1], b=yacc[:, db, :]:
                                   e.scalar_tensor_tensor(o, a, sc, b,
                                                          ALU.mult, ALU.add),
                               [war["yacc"][db], t_al, t_ld])
                t_sz = prog.op("scalar",
                               lambda e, o=zc[:, db, :], i=zc[:, db, :]:
                                   e.activation(o, i, AF.Silu),
                               [t_z])
                t_g = prog.op("vector",
                              lambda e, o=ygb[:, db, :], a=yacc[:, db, :],
                                     b=zc[:, db, :]:
                                  e.tensor_mul(o, a, b),
                              [t_yd, t_sz, war["ygb"][db]])
                out_toks.append(t_g)
                war["yacc"][db] = t_g
            t_out = prog.dma1("sync",
                              lambda e, o=yg_view[:, :, t0:t0 + LC], i=ygb[:, :, :]:
                                  e.dma_start(o, i), out_toks)
            for db in range(4):
                war["ygb"][db] = t_out
            war["zc"] = prog.token("vector")
            war["dtc"] = prog.token("vector")
            war["xcc"] = prog.token("vector")
            war["uc"] = prog.token("vector")
        prog.barrier()
        block = stack.enter_context(nc.Block())
        prog.install(block)
    return nc



def build_launch6():
    nc = bass.Bass("TRN2", target_bir_lowering=False)
    ygT = nc.dram_tensor("ygT", [4096, TC], BF16, kind="ExternalInput").ap()
    opT = nc.dram_tensor("opT", [4096, D], F32, kind="ExternalInput").ap()
    xn2T = nc.dram_tensor("xn2T", [D, TC], F32, kind="ExternalInput").ap()
    ffpT = nc.dram_tensor("ffpT", [D, 8192], F32, kind="ExternalInput").ap()
    ffoT = nc.dram_tensor("ffoT", [4096, D], F32, kind="ExternalInput").ap()
    l2w = nc.dram_tensor("l2w", [PART, 16], F32, kind="ExternalInput").ap()
    l2b = nc.dram_tensor("l2b", [PART, 16], F32, kind="ExternalInput").ap()
    o_acc6 = nc.dram_tensor("o_acc6", [D, TC], F32, kind="ExternalOutput").ap()
    hsw_d = nc.dram_tensor("hsw_d", [4096, TC], BF16)

    stack = ExitStack()
    with stack:
        prog = Prog(nc, stack)
        psum = PsumPool(prog, stack, 8)
        misc = Misc(prog, stack, TC)
        sb = lambda n_, sh, dt: stack.enter_context(nc.sbuf_tensor(n_, sh, dt))

        og = sb("og", [PART, 16, TC], F32)
        h2b = sb("h2b", [PART, 16, TC], BF16)
        l2w_sb = sb("l2w_sb", [PART, 16], F32)
        l2b_sb = sb("l2b_sb", [PART, 16], F32)

        xn2_view = xn2T.rearrange("(c p) t -> p c t", p=PART)
        yg_view = ygT.rearrange("(kc p) t -> p kc t", p=PART)
        hsw_view = hsw_d.rearrange("(c p) t -> p c t", p=PART)

        gl = prog.dma_group()
        gl.dma("sync", lambda e: e.dma_start(og[:, :, :], xn2_view))
        gl.dma("sync", lambda e: e.dma_start(l2w_sb[:, :], l2w))
        gl.dma("sync", lambda e: e.dma_start(l2b_sb[:, :], l2b))
        t_in = gl.seal()

        _kbn = [0]

        def kblocked_acc(wt_dram, rhs_dram_view, K, acc_dep, nkb=4):
            kc = K // PART
            kpb = kc // nkb
            _kbn[0] += 1
            wview = wt_dram.rearrange("(kc p) m -> p kc m", p=PART)
            with nc.sbuf_tensor(f"kb_rhs{_kbn[0]}", [PART, kpb, TC], BF16) as rhsblk, \
                 nc.sbuf_tensor(f"kb_w{_kbn[0]}", [PART, kpb, 1024], BF16) as wblk:
                for kb in range(nkb):
                    t_r = prog.dma1("sync",
                                    lambda e, o=rhsblk[:, :, :],
                                           i=rhs_dram_view[:, kb * kpb:(kb + 1) * kpb, :]:
                                        e.dma_start(o, i),
                                    [prog.token("tensor") if kb else acc_dep])
                    for mh in range(2):
                        t_w = prog.dma1(
                            "gpsimd",
                            lambda e, o=wblk[:, :, :],
                                   i=wview[:, kb * kpb:(kb + 1) * kpb,
                                           mh * 1024:(mh + 1) * 1024]:
                                e.dma_start(o, i),
                            [prog.token("tensor")])
                        for mj in range(8):
                            mi = mh * 8 + mj
                            for ti in range(2):
                                t0 = ti * 512
                                bank, war, bi = psum.acquire()
                                mm = None
                                for kj in range(kpb):
                                    mm = prog.op(
                                        "tensor",
                                        lambda e, o=bank[:, :],
                                               l=wblk[:, kj, mj * 128:(mj + 1) * 128],
                                               r=rhsblk[:, kj, t0:t0 + 512],
                                               s=(kj == 0), p=(kj == kpb - 1):
                                            e.matmul(o, l, r, start=s, stop=p),
                                        [t_w, t_r] + ([war] if kj == 0 else []),
                                        inc=(kj == kpb - 1))
                                tadd = prog.op(
                                    "vector",
                                    lambda e, o=og[:, mi, t0:t0 + 512], a=bank[:, :],
                                           b=og[:, mi, t0:t0 + 512]:
                                        e.tensor_add(o, a, b), [mm, t_in])
                                psum.release(bi, tadd)
                prog.barrier()

        kblocked_acc(opT, yg_view, 4096, None)
        t_h2 = layernorm(prog, psum, misc, x=og, nch=16, T=TC,
                         out_bf16=h2b, gamma=l2w_sb, beta=l2b_sb, eps=1e-5,
                         use_mean=True, dep=prog.token("vector"))
        with ExitStack() as st2:
            wba = st2.enter_context(nc.sbuf_tensor("wba", [PART, 16, 2, 128], BF16))
            wbg = st2.enter_context(nc.sbuf_tensor("wbg", [PART, 16, 2, 128], BF16))
            gt6 = st2.enter_context(nc.sbuf_tensor("gt6", [PART, 2, 512], F32))
            wsa = WStreamer(prog, wba, ffpT[:, 0:4096],
                            D, 4096, mc=128, init_war=None)
            wsg = WStreamer(prog, wbg, ffpT[:, 4096:8192], D, 4096, mc=128,
                            init_war=None)
            hstage = StagePool(prog, st2, "hst6", 2, (PART, 512), BF16)
            gw = [None, None]
            for mj in range(32):
                for ti in range(2):
                    t0 = ti * 512
                    psA, warA, biA = psum.acquire()
                    psG, warG, biG = psum.acquire()
                    mmA = mmG = None
                    for ki in range(16):
                        mmA = prog.op(
                            "tensor",
                            lambda e, o=psA[:, :], l=wsa.lhsT(mj, ki),
                                   r=h2b[:, ki, t0:t0 + 512], s=(ki == 0),
                                   p=(ki == 15):
                                e.matmul(o, l, r, start=s, stop=p),
                            [wsa.dep(mj, ki), t_h2] + ([warA] if ki == 0 else []),
                            inc=(ki == 15))
                        mmG = prog.op(
                            "tensor",
                            lambda e, o=psG[:, :], l=wsg.lhsT(mj, ki),
                                   r=h2b[:, ki, t0:t0 + 512], s=(ki == 0),
                                   p=(ki == 15):
                                e.matmul(o, l, r, start=s, stop=p),
                            [wsg.dep(mj, ki)] + ([warG] if ki == 0 else []),
                            inc=(ki == 15))
                    sl = ti
                    t_si = prog.op("scalar",
                                   lambda e, o=gt6[:, sl, :], i=psG[:, :]:
                                       e.activation(o, i, AF.Silu),
                                   [mmG, gw[sl]])
                    psum.release(biG, t_si)
                    ht_, hwar, hi = hstage.acquire()
                    t_pr = prog.op("vector",
                                   lambda e, o=ht_[:, :], a=psA[:, :],
                                          b=gt6[:, sl, :]:
                                       e.tensor_mul(o, a, b),
                                   [t_si, mmA, hwar])
                    gw[sl] = t_pr
                    psum.release(biA, t_pr)
                    dtok = prog.dma1("gpsimd",
                                     lambda e, o=hsw_view[:, mj, t0:t0 + 512],
                                            i=ht_[:, :]:
                                         e.dma_start(o, i), [t_pr])
                    hstage.release(hi, dtok)
                wsa.m_done(mj, prog.token("tensor"))
                wsg.m_done(mj, prog.token("tensor"))
        prog.barrier()
        kblocked_acc(ffoT, hsw_view, 4096, None)
        a6_view = o_acc6.rearrange("(c p) t -> p c t", p=PART)
        go = prog.dma_group()
        for ch in range(16):
            go.dma("sync", lambda e, ch=ch: e.dma_start(
                a6_view[:, ch, :], og[:, ch, :]), [prog.token("vector")])
        go.seal()
        prog.barrier()
        block = stack.enter_context(nc.Block())
        prog.install(block)
    return nc



def build_launch7():
    nc = bass.Bass("TRN2", target_bir_lowering=False)
    xn2b = nc.dram_tensor("xn2b", [D, TC], BF16, kind="ExternalInput").ap()
    x1T = nc.dram_tensor("x1T", [D, TC], F32, kind="ExternalInput").ap()
    a6T = nc.dram_tensor("a6T", [D, TC], F32, kind="ExternalInput").ap()
    w1t = nc.dram_tensor("w1t", [D, 5632], F32, kind="ExternalInput").ap()
    w2t = nc.dram_tensor("w2t", [D, 5632], F32, kind="ExternalInput").ap()
    w3t = nc.dram_tensor("w3t", [5632, D], F32, kind="ExternalInput").ap()
    o_y = nc.dram_tensor("o_y", [D, TC], F32, kind="ExternalOutput").ap()
    hid_d = nc.dram_tensor("hid_d", [5632, TC], BF16)

    H = 5632
    HC = 44

    stack = ExitStack()
    with stack:
        prog = Prog(nc, stack)
        psum = PsumPool(prog, stack, 8)
        sb = lambda n_, sh, dt: stack.enter_context(nc.sbuf_tensor(n_, sh, dt))

        acc = sb("acc7", [PART, 16, TC], F32)
        xnb = sb("xnb7", [PART, 16, TC], BF16)
        ones_col = sb("ones_c7", [PART, 1], F32)
        ones_row = sb("ones_r7", [1, PART], F32)
        redc = sb("redc", [PART, 80], F32)
        ths = sb("ths", [PART, 8], F32)

        t_o1 = prog.op("vector", lambda e: e.memset(ones_col[:, :], 1.0))
        t_o2 = prog.op("vector", lambda e: e.memset(ones_row[:, :], 1.0))

        gl = prog.dma_group()
        gl.dma("sync", lambda e: e.dma_start(
            acc[:, :, :], a6T.rearrange("(c p) t -> p c t", p=PART)))
        gl.dma("sync", lambda e: e.dma_start(
            xnb[:, :, :], xn2b.rearrange("(c p) t -> p c t", p=PART)))
        t_in = gl.seal()

        views = {
            "w1": (w1t.rearrange("(kc p) m -> p kc m", p=PART), 16, H),
            "w2": (w2t.rearrange("(kc p) m -> p kc m", p=PART), 16, H),
            "w3": (w3t.rearrange("(kc p) m -> p kc m", p=PART), HC, D),
        }
        numel = {"w1": D * H, "w2": D * H, "w3": D * H}

        red_cols = {}
        col_i = 0
        with ExitStack() as st2:
            gbuf = [st2.enter_context(nc.sbuf_tensor(f"gb{i}", [PART, H], F32))
                    for i in range(2)]
            gwar = [None, None]
            for wname in ("w1", "w2", "w3"):
                view, kc, M = views[wname]
                cols = []
                for ki in range(kc):
                    slq = col_i % 2
                    t_l = prog.dma1("sync",
                                    lambda e, o=gbuf[slq][:, 0:M], i=view[:, ki, :]:
                                        e.dma_start(o, i), [gwar[slq]])
                    t_r = prog.op("vector",
                                  lambda e, o=redc[:, col_i:col_i + 1],
                                         i=gbuf[slq][:, 0:M]:
                                      e.tensor_reduce(o, i, AX, ALU.add,
                                                      apply_absolute_value=True),
                                  [t_l])
                    gwar[slq] = t_r
                    cols.append(col_i)
                    col_i += 1
                red_cols[wname] = cols
            for wi, wname in enumerate(("w1", "w2", "w3")):
                cols = red_cols[wname]
                bank, war, bi = psum.acquire()
                mm = prog.op("tensor",
                             lambda e, o=bank[0:1, 0:len(cols)], l=ones_col[:, :],
                                    r=redc[:, cols[0]:cols[-1] + 1]:
                                 e.matmul(o, l, r, start=True, stop=True),
                             [prog.token("vector"), t_o1, war])
                t_s = prog.op("vector",
                              lambda e, o=redc[0:1, 76 + wi:77 + wi],
                                     i=bank[0:1, 0:len(cols)]:
                                  e.tensor_reduce(o, i, AX, ALU.add), [mm])
                psum.release(bi, t_s)
                t_t = prog.op("vector",
                              lambda e, o=redc[0:1, 76 + wi:77 + wi],
                                     i=redc[0:1, 76 + wi:77 + wi],
                                     s1=1.0 / numel[wname], s2=1e-8:
                                  e.tensor_scalar(o, i, s1, s2, ALU.mult, ALU.add),
                              [t_s])
                t_t = prog.op("vector",
                              lambda e, o=redc[0:1, 76 + wi:77 + wi],
                                     i=redc[0:1, 76 + wi:77 + wi]:
                                  e.tensor_scalar_mul(o, i, 0.5), [t_t])
                bank, war, bi = psum.acquire()
                mm = prog.op("tensor",
                             lambda e, o=bank[:, 0:1], l=ones_row[0:1, :],
                                    r=redc[0:1, 76 + wi:77 + wi]:
                                 e.matmul(o, l, r, start=True, stop=True),
                             [t_t, t_o2, war])
                t_c = prog.op("vector",
                              lambda e, o=ths[:, 2 * wi:2 * wi + 1], i=bank[:, 0:1]:
                                  e.tensor_copy(o, i), [mm])
                t_c2 = prog.op("vector",
                               lambda e, o=ths[:, 2 * wi + 1:2 * wi + 2],
                                      i=ths[:, 2 * wi:2 * wi + 1]:
                                   e.tensor_scalar_mul(o, i, -1.0), [t_c])
                psum.release(bi, t_c2)
        prog.barrier()
        th_tok = prog.token("vector")

        def quantize(dst, src, wi, deps):
            t1 = prog.op("vector",
                         lambda e, o=dst, i=src, s=ths[:, 2 * wi + 1:2 * wi + 2]:
                             e.tensor_scalar(o, i, s, None, ALU.is_lt),
                         deps + [th_tok])
            t2 = prog.op("vector",
                         lambda e, o=dst, i=src, s=ths[:, 2 * wi:2 * wi + 1], b=dst:
                             e.scalar_tensor_tensor(o, i, s, b, ALU.is_gt,
                                                    ALU.subtract),
                         [t1])
            return t2

        hid_view = hid_d.rearrange("(c p) t -> p c t", p=PART)
        with ExitStack() as st3:
            wf = [st3.enter_context(nc.sbuf_tensor(f"wf{i}", [PART, 16, 128], F32))
                  for i in range(2)]
            qb = [st3.enter_context(nc.sbuf_tensor(f"qb{i}", [PART, 16, 128], BF16))
                  for i in range(2)]
            sg7 = st3.enter_context(nc.sbuf_tensor("sg7", [PART, 2, 512], F32))
            hstage = StagePool(prog, st3, "hst7", 2, (PART, 512), BF16)
            wfwar = [None, None]
            qbwar = [None, None]
            sgwar = [None, None]
            for mj in range(HC):
                q_toks = {}
                for wi, (wname, wt) in enumerate((("w1", w1t), ("w2", w2t))):
                    slq = wi
                    view = views[wname][0]
                    t_l = prog.dma1("sync",
                                    lambda e, o=wf[slq][:, :, :],
                                           i=view[:, :, mj * 128:(mj + 1) * 128]:
                                        e.dma_start(o, i), [wfwar[slq]])
                    t_q = quantize(qb[slq][:, :, :], wf[slq][:, :, :], wi,
                                   [t_l, qbwar[slq]])
                    wfwar[slq] = t_q
                    q_toks[wi] = t_q
                for ti in range(2):
                    t0 = ti * 512
                    ps1, war1, bi1 = psum.acquire()
                    ps2, war2, bi2 = psum.acquire()
                    mm1 = mm2 = None
                    for ki in range(16):
                        mm1 = prog.op(
                            "tensor",
                            lambda e, o=ps1[:, :], l=qb[0][:, ki, :],
                                   r=xnb[:, ki, t0:t0 + 512], s=(ki == 0),
                                   p=(ki == 15):
                                e.matmul(o, l, r, start=s, stop=p),
                            [q_toks[0], t_in] + ([war1] if ki == 0 else []),
                            inc=(ki == 15))
                        mm2 = prog.op(
                            "tensor",
                            lambda e, o=ps2[:, :], l=qb[1][:, ki, :],
                                   r=xnb[:, ki, t0:t0 + 512], s=(ki == 0),
                                   p=(ki == 15):
                                e.matmul(o, l, r, start=s, stop=p),
                            [q_toks[1]] + ([war2] if ki == 0 else []),
                            inc=(ki == 15))
                    t_sg = prog.op("scalar",
                                   lambda e, o=sg7[:, ti, :], i=ps1[:, :]:
                                       e.activation(o, i, AF.Sigmoid),
                                   [mm1, sgwar[ti]])
                    psum.release(bi1, t_sg)
                    ht_, hwar, hi = hstage.acquire()
                    t_h = prog.op("vector",
                                  lambda e, o=ht_[:, :], a=ps2[:, :],
                                         b=sg7[:, ti, :]:
                                      e.tensor_mul(o, a, b),
                                  [t_sg, mm2, hwar])
                    sgwar[ti] = t_h
                    psum.release(bi2, t_h)
                    dtok = prog.dma1("gpsimd",
                                     lambda e, o=hid_view[:, mj, t0:t0 + 512],
                                            i=ht_[:, :]:
                                         e.dma_start(o, i), [t_h])
                    hstage.release(hi, dtok)
                qbwar[0] = prog.token("tensor")
                qbwar[1] = qbwar[0]
        prog.barrier()
        with ExitStack() as st4:
            hblk = st4.enter_context(nc.sbuf_tensor("hblk", [PART, 11, TC], BF16))
            w3f = st4.enter_context(nc.sbuf_tensor("w3f", [PART, 11, 1024], F32))
            q3b = st4.enter_context(nc.sbuf_tensor("q3b", [PART, 11, 1024], BF16))
            v3, _, _ = views["w3"]
            for kb in range(4):
                t_hb = prog.dma1("sync",
                                 lambda e, o=hblk[:, :, :],
                                        i=hid_view[:, kb * 11:(kb + 1) * 11, :]:
                                     e.dma_start(o, i),
                                 [prog.token("tensor") if kb else None])
                for mh in range(2):
                    t_w3 = prog.dma1(
                        "scalar",
                        lambda e, o=w3f[:, :, :],
                               i=v3[:, kb * 11:(kb + 1) * 11,
                                    mh * 1024:(mh + 1) * 1024]:
                            e.dma_start(o, i),
                        [prog.token("vector"), prog.token("tensor")])
                    t_q3 = quantize(q3b[:, :, :], w3f[:, :, :], 2, [t_w3])
                    for mj in range(8):
                        mi = mh * 8 + mj
                        for ti in range(2):
                            t0 = ti * 512
                            bank, war, bi = psum.acquire()
                            mm = None
                            for kj in range(11):
                                mm = prog.op(
                                    "tensor",
                                    lambda e, o=bank[:, :],
                                           l=q3b[:, kj, mj * 128:(mj + 1) * 128],
                                           r=hblk[:, kj, t0:t0 + 512],
                                           s=(kj == 0), p=(kj == 10):
                                        e.matmul(o, l, r, start=s, stop=p),
                                    [t_q3, t_hb] + ([war] if kj == 0 else []),
                                    inc=(kj == 10))
                            tadd = prog.op(
                                "vector",
                                lambda e, o=acc[:, mi, t0:t0 + 512], a=bank[:, :],
                                       b=acc[:, mi, t0:t0 + 512]:
                                    e.tensor_add(o, a, b), [mm, t_in])
                            psum.release(bi, tadd)
        prog.barrier()
        with ExitStack() as st5:
            xst = [st5.enter_context(nc.sbuf_tensor(f"xst{i}", [PART, TC], F32))
                   for i in range(2)]
            x1_view = x1T.rearrange("(c p) t -> p c t", p=PART)
            y_view = o_y.rearrange("(c p) t -> p c t", p=PART)
            xw = [None, None]
            for ch in range(16):
                slx = ch % 2
                t_x = prog.dma1("sync",
                                lambda e, o=xst[slx][:, :], i=x1_view[:, ch, :]:
                                    e.dma_start(o, i), [xw[slx]])
                t_a = prog.op("vector",
                              lambda e, o=acc[:, ch, :], a=xst[slx][:, :],
                                     b=acc[:, ch, :]:
                                  e.tensor_add(o, a, b),
                              [t_x, prog.token("vector")])
                xw[slx] = t_a
                prog.dma1("sync",
                          lambda e, o=y_view[:, ch, :], i=acc[:, ch, :]:
                              e.dma_start(o, i), [t_a])
        prog.barrier()
        block = stack.enter_context(nc.Block())
        prog.install(block)
    return nc



def _ct(a):
    return np.ascontiguousarray(a)


def kernel_unfused(**inp):
    x = inp["L0_attn_x"][0]
    nc1 = build_launch1()
    m1 = []
    for c in range(NCORES):
        sl = slice(c * TC, (c + 1) * TC)
        m1.append({
            "xT": _ct(x[sl].T),
            "wq0t": _ct(inp["L0_attn_Wq"].T), "wk0t": _ct(inp["L0_attn_Wk"].T),
            "wv0t": _ct(inp["L0_attn_Wv"].T),
            "wq1t": _ct(inp["L0_ffn_Wq"].T), "wk1t": _ct(inp["L0_ffn_Wk"].T),
            "wv1t": _ct(inp["L0_ffn_Wv"].T),
            "wo1n": inp["L0_ffn_Wo"], "a_n": inp["L0_ffn_A"],
            "ln1v": _ct(inp["L0_ffn_ln1_w"].reshape(16, 128).T),
            "ln2v": _ct(inp["L0_ffn_ln2_w"].reshape(16, 128).T),
        })
    r1 = run_bass_kernel_spmd(nc1, m1, core_ids=list(range(NCORES))).results

    cat1 = lambda key: np.concatenate([r[key] for r in r1], axis=1)
    q0f, k0f, v0f = cat1("o_q0"), cat1("o_k0"), cat1("o_v0")
    q1f, k1f, v1f = cat1("o_q1"), cat1("o_k1"), cat1("o_v1")
    G = r1[0]["o_g"]
    kw_, tw_, pw_ = mask_patterns("window")
    kc2, tc2, pc2 = mask_patterns("causal")
    mw_in = _ct(np.stack(list(pw_), axis=1))
    mc_in = _ct(np.stack(list(pc2), axis=1))

    def tiles_k(kt_full):
        return _ct(kt_full.reshape(128, 64, 128).transpose(1, 0, 2))

    def tiles_v(kt_full):
        return _ct(kt_full.T.reshape(64, 128, 128))

    nc2 = build_launch2()
    m2 = []
    for c in range(NCORES):
        h0, h1 = 2 * c, 2 * c + 1
        kv = c // 2
        m2.append({
            "q0h": _ct(q0f.reshape(16, 128, T_FULL)[[h0, h1]]),
            "k0h": tiles_k(k0f[kv * 128:(kv + 1) * 128])[None],
            "v0h": tiles_v(v0f[kv * 128:(kv + 1) * 128])[None],
            "q1h": _ct(q1f.reshape(16, 128, T_FULL)[[h0, h1]]),
            "k1h": np.stack([tiles_k(k1f[h * 128:(h + 1) * 128]) for h in (h0, h1)]),
            "v1h": np.stack([tiles_v(v1f[h * 128:(h + 1) * 128]) for h in (h0, h1)]),
            "gsl": _ct(G.reshape(16, 128, 16)[[h0, h1]]),
            "masks_w": mw_in, "masks_c": mc_in,
        })
    r2 = run_bass_kernel_spmd(nc2, m2, core_ids=list(range(NCORES))).results

    o0_full = np.empty((D, T_FULL), dtype=BF)
    for c in range(NCORES):
        for h in range(2):
            o0_full[(2 * c + h) * 128:(2 * c + h + 1) * 128] = r2[c]["o_o0"][h]
    nc3 = build_launch3()
    m3 = []
    for c in range(NCORES):
        sl = slice(c * TC, (c + 1) * TC)
        m3.append({
            "xT": _ct(x[sl].T),
            "xnT": _ct(r1[c]["o_xn"]),
            "o0tok": _ct(o0_full[:, sl]),
            "modT": _ct(r2[c]["o_mod"][:, sl]),
            "hT": _ct(r1[c]["o_h"]),
            "wo0t": _ct(inp["L0_attn_Wo"].T),
            "w_int": _ct(inp["L0_ffn_W_in_base"].T),
            "w_upt": _ct(inp["L0_ffn_W_up_base"].T),
            "w_downt": _ct(inp["L0_ffn_W_down"].T),
            "a_n": inp["L0_ffn_A"], "b_n": inp["L0_ffn_B"],
            "l1w": _ct(inp["L1_attn_ln1_w"].reshape(16, 128).T),
            "l1b": _ct(inp["L1_attn_ln1_b"].reshape(16, 128).T),
        })
    r3 = run_bass_kernel_spmd(nc3, m3, core_ids=list(range(NCORES))).results

    u_full = np.concatenate([r["o_u"] for r in r3], axis=1)
    ipT = _ct(inp["L1_attn_in_proj_w"].T)
    nc4 = build_launch4()
    m4 = []
    for c in range(NCORES):
        cs = slice(512 * c, 512 * (c + 1))
        m4.append({
            "uT": u_full,
            "ipT": _ct(np.concatenate(
                [ipT[:, cs], ipT[:, 4096 + 512 * c:4096 + 512 * (c + 1)]], 1)),
            "convw": _ct(inp["L1_attn_conv_w"][cs, 0, :]
                         .reshape(4, 128, 4).transpose(1, 0, 2)),
            "convb": _ct(inp["L1_attn_conv_b"][cs].reshape(4, 128).T),
            "xpT": _ct(inp["L1_attn_x_proj_w"].T[cs]),
            "dtwT": _ct(inp["L1_attn_dt_proj_w"].T[:, cs]),
            "dtb": _ct(inp["L1_attn_dt_proj_b"][cs].reshape(4, 128).T),
        })
    r4 = run_bass_kernel_spmd(nc4, m4, core_ids=list(range(NCORES))).results

    nc5 = build_launch5()
    m5 = []
    for c in range(NCORES):
        cs = slice(512 * c, 512 * (c + 1))
        m5.append({
            "xcS": r4[c]["o_xc"], "zS": r4[c]["o_z"], "dtS": r4[c]["o_dt"],
            "bcT": r4[c]["o_bc"],
            "alog": _ct(inp["L1_attn_A_log"][cs].reshape(4, 128, 16)
                        .transpose(1, 0, 2)),
            "dsk": _ct(inp["L1_attn_D_skip"][cs].reshape(4, 128).T),
        })
    r5 = run_bass_kernel_spmd(nc5, m5, core_ids=list(range(NCORES))).results

    yg_full = np.concatenate([r["o_yg"] for r in r5], axis=0)
    nc6 = build_launch6()
    m6 = []
    for c in range(NCORES):
        sl = slice(c * TC, (c + 1) * TC)
        m6.append({
            "ygT": _ct(yg_full[:, sl]),
            "opT": _ct(inp["L1_attn_out_proj_w"].T),
            "xn2T": r3[c]["o_xn2"],
            "ffpT": _ct(inp["L1_attn_ff_proj_w"].T),
            "ffoT": _ct(inp["L1_attn_ff_out_w"].T),
            "l2w": _ct(inp["L1_attn_ln2_w"].reshape(16, 128).T),
            "l2b": _ct(inp["L1_attn_ln2_b"].reshape(16, 128).T),
        })
    r6 = run_bass_kernel_spmd(nc6, m6, core_ids=list(range(NCORES))).results

    nc7 = build_launch7()
    m7 = []
    for c in range(NCORES):
        m7.append({
            "xn2b": r3[c]["o_xn2b"],
            "x1T": r3[c]["o_x1"],
            "a6T": r6[c]["o_acc6"],
            "w1t": _ct(inp["L1_ffn_w1"].T),
            "w2t": _ct(inp["L1_ffn_w2"].T),
            "w3t": _ct(inp["L1_ffn_w3"].T),
        })
    r7 = run_bass_kernel_spmd(nc7, m7, core_ids=list(range(NCORES))).results

    y = np.concatenate([r["o_y"] for r in r7], axis=1)
    return _ct(y.T)[None]



def build_fused():
    nc = bass.Bass("TRN2", target_bir_lowering=False)
    ei = lambda n_, sh, dt=F32: nc.dram_tensor(n_, sh, dt, kind="ExternalInput").ap()
    dscr = lambda n_, sh, dt: nc.dram_tensor(n_, sh, dt).ap()

    xT = ei("xT", [D, TC])
    wq0sl = ei("wq0sl", [D, 256]); wk0sl = ei("wk0sl", [D, PART])
    wv0sl = ei("wv0sl", [D, PART])
    wq1sl = ei("wq1sl", [D, 256]); wk1sl = ei("wk1sl", [D, 256])
    wv1sl = ei("wv1sl", [D, 256])
    wo1sl = ei("wo1sl", [D, 256]); a_n = ei("a_n", [D, 16])
    ln1v = ei("ln1v", [PART, 16]); ln2v = ei("ln2v", [PART, 16])
    kw_, tw_, pw_ = mask_patterns("window")
    kc2, tc2, pc2 = mask_patterns("causal")
    masks_w = ei("masks_w", [PART, len(pw_), 512], BF16)
    masks_c = ei("masks_c", [PART, len(pc2), 512], BF16)
    wo0t = ei("wo0t", [D, D])
    w_int = ei("w_int", [D, 8192]); w_upt = ei("w_upt", [D, 8192])
    w_downt = ei("w_downt", [8192, D]); b_n = ei("b_n", [16, 8192])
    l1w = ei("l1w", [PART, 16]); l1b = ei("l1b", [PART, 16])
    ipT = ei("ipT", [D, 1024])
    convw = ei("convw", [PART, 4, 4]); convb = ei("convb", [PART, 4])
    xpT = ei("xpT", [512, 160]); dtwT = ei("dtwT", [PART, 512])
    dtb = ei("dtb", [PART, 4])
    alog = ei("alog", [PART, 4, 16]); dsk = ei("dsk", [PART, 4])
    opT = ei("opT", [4096, D]); ffpT = ei("ffpT", [D, 8192]); ffoT = ei("ffoT", [4096, D])
    l2w = ei("l2w", [PART, 16]); l2b = ei("l2b", [PART, 16])
    w1t = ei("w1t", [D, 5632]); w2t = ei("w2t", [D, 5632]); w3t = ei("w3t", [5632, D])
    o_y = nc.dram_tensor("o_y", [D, TC], F32, kind="ExternalOutput").ap()

    xn_d = dscr("xn_d", [D, TC], F32)
    h_d = dscr("h_d", [D, TC], BF16)
    xn_in = dscr("xn_in", [D, TC], BF16); xng = dscr("xng", [NCORES * D, TC], BF16)
    x1_in = dscr("x1_in", [D, TC], BF16); xlg = dscr("xlg", [NCORES * D, TC], BF16)
    u_in = dscr("u_in", [D, TC], BF16); ug = dscr("ug", [NCORES * D, TC], BF16)
    o0_in = dscr("o0_in", [256, T_FULL], BF16)
    o0g = dscr("o0g", [D, T_FULL], BF16)
    yg_in = dscr("yg_in", [512, T_FULL], BF16)
    ygall = dscr("ygall", [4096, T_FULL], BF16)
    mod_in = dscr("mod_in", [16, T_FULL], F32)
    mod_out = dscr("mod_out", [16, T_FULL], F32)
    aq0o = dscr("aq0o", [D, TC], BF16)
    ak0o = dscr("ak0o", [1024, TC], BF16)
    av0o = dscr("av0o", [1024, TC], BF16)
    aq1o = dscr("aq1o", [D, TC], BF16)
    ak1o = dscr("ak1o", [D, TC], BF16)
    av1o = dscr("av1o", [D, TC], BF16)
    v0tok = dscr("v0tok", [64, PART, PART], BF16)
    v1tok = dscr("v1tok", [2, 64, PART, PART], BF16)
    ccxi = dscr("ccxi", [160, T_FULL], F32); ccxo = dscr("ccxo", [160, T_FULL], F32)
    xc_d = dscr("xc_d", [512, T_FULL], F32)
    z_d = dscr("z_d", [512, T_FULL], F32)
    dt_d = dscr("dt_d", [512, T_FULL], F32)
    prod_d = dscr("prod_f", [8192, TC], BF16)
    hsw_d = dscr("hsw_f", [4096, TC], BF16)
    hid_d = dscr("hid_f", [5632, TC], BF16)
    x1_d = dscr("x1_d", [D, TC], F32)
    xn2_d = dscr("xn2_d", [D, TC], F32)
    xn2b_d = dscr("xn2b_d", [D, TC], BF16)

    RG = [list(range(NCORES))]

    stack = ExitStack()
    with stack:
        prog = Prog(nc, stack)
        psum = PsumPool(prog, stack, 7)
        misc = Misc(prog, stack, TC)
        sbo = lambda n_, sh, dt: stack.enter_context(nc.sbuf_tensor(n_, sh, dt))
        g_sb = sbo("g_sb", [PART, 2, 16], BF16)

        def gather(in_ap, out_ap, deps):
            return prog.cc(
                lambda e, i=in_ap, o=out_ap: e.collective_compute(
                    "AllGather", ALU.bypass, replica_groups=RG,
                    ins=[i.opt()], outs=[o.opt()]), deps)

        def allreduce(in_ap, out_ap, deps):
            return prog.cc(
                lambda e, i=in_ap, o=out_ap: e.collective_compute(
                    "AllReduce", ALU.add, replica_groups=RG,
                    ins=[i.opt()], outs=[o.opt()]), deps)

        with ExitStack() as st:
            sb = lambda n_, sh, dt: st.enter_context(nc.sbuf_tensor(n_, sh, dt))
            x_sb = sb("x_sb", [PART, 16, TC], F32)
            act_bf = sb("act_bf", [PART, 16, TC], BF16)
            g1v = sb("g1v", [PART, 16], F32)
            g2v = sb("g2v", [PART, 16], F32)
            wbuf = sb("wbuf", [PART, 16, 2, 256], BF16)
            stage = StagePool(prog, st, "st", 4, (PART, 512), BF16)

            gl = prog.dma_group()
            gl.dma("sync", lambda e: e.dma_start(
                x_sb[:, :, :], xT.rearrange("(c p) t -> p c t", p=PART)))
            gl.dma("sync", lambda e: e.dma_start(g1v[:, :], ln1v[:, :]))
            gl.dma("sync", lambda e: e.dma_start(g2v[:, :], ln2v[:, :]))
            t_in = gl.seal()

            t_xn = layernorm(prog, psum, misc, x=x_sb, nch=16, T=TC,
                             out_f32=x_sb, out_bf16=act_bf, eps=1e-5,
                             use_mean=True, dep=t_in)
            xn_view = xn_d.rearrange("(c p) t -> p c t", p=PART)
            gxn = prog.dma_group()
            for ch in range(16):
                gxn.dma("scalar", lambda e, ch=ch: e.dma_start(
                    xn_view[:, ch, :], x_sb[:, ch, :]), [t_xn])
            t_oxn = gxn.seal()

            xn_in_v = xn_in.rearrange("(c p) t -> p c t", p=PART)
            gi1 = prog.dma_group()
            for ch in range(16):
                gi1.dma("scalar", lambda e, ch=ch: e.dma_start(
                    xn_in_v[:, ch, :], act_bf[:, ch, :]), [t_xn])
            t_gi1 = gi1.seal()
            cc_xn = gather(xn_in, xng, [t_gi1])

            wsg_ = WStreamer(prog, wbuf, wo1sl, D, 256, init_war=None)
            a_sb, t_a = load_weight_bf16(prog, st, "a_sb_f", a_n, D, 16)

            def g_evac(mi, ti, ps, mm_tok):
                return prog.op("vector",
                               lambda e, d=g_sb[:, mi, :], p=ps:
                                   e.tensor_copy(d, p), [mm_tok])

            gemm(prog, psum, M=256, K=D, T=16, tile_t=16,
                 lhsT=wsg_.lhsT, rhs=lambda ki, ti: a_sb[:, ki, :],
                 lhs_dep=wsg_.dep, rhs_dep=lambda ki, ti, t=t_a: t,
                 evac=g_evac, m_done=wsg_.m_done)

            prog.barrier()
            t_x1 = layernorm(prog, psum, misc, x=x_sb, nch=16, T=TC,
                             out_bf16=act_bf, gamma=g1v, eps=1e-6,
                             use_mean=False, dep=t_oxn)
            x1_in_v = x1_in.rearrange("(c p) t -> p c t", p=PART)
            gi2 = prog.dma_group()
            for ch in range(16):
                gi2.dma("scalar", lambda e, ch=ch: e.dma_start(
                    x1_in_v[:, ch, :], act_bf[:, ch, :]), [t_x1])
            t_gi2 = gi2.seal()
            cc_x1 = gather(x1_in, xlg, [t_gi2])

            prog.barrier()
            t_h = layernorm(prog, psum, misc, x=x_sb, nch=16, T=TC,
                            out_bf16=act_bf, gamma=g2v, eps=1e-6,
                            use_mean=False, dep=None)
            h_view = h_d.rearrange("(c p) t -> p c t", p=PART)
            gh = prog.dma_group()
            for ch in range(16):
                gh.dma("sync", lambda e, ch=ch: e.dma_start(
                    h_view[:, ch, :], act_bf[:, ch, :]), [t_h])
            gh.seal()
            prog.barrier()

        with ExitStack() as st:
            sb = lambda n_, sh, dt: st.enter_context(nc.sbuf_tensor(n_, sh, dt))
            wsl = {}
            glw = prog.dma_group(sw=True)
            for nm, src, M in (("q0", wq0sl, 256), ("k0", wk0sl, 128),
                               ("v0", wv0sl, 128), ("q1", wq1sl, 256),
                               ("k1", wk1sl, 256), ("v1", wv1sl, 256)):
                wsl[nm], _ = load_weight_bf16(prog, st, f"wsl_{nm}", src, D, M,
                                              g=glw)
            t_w = glw.seal()
            xbuf = [sb("xb75_0", [PART, 16, 512], BF16),
                    sb("xb75_1", [PART, 16, 512], BF16)]
            stage75 = StagePool(prog, st, "st75", 4, (PART, 512), BF16)
            xng_v = xng.rearrange("(r c p) t -> p r c t", p=PART, c=16)
            xlg_v = xlg.rearrange("(r c p) t -> p r c t", p=PART, c=16)
            outs0 = (("q0", aq0o, 2), ("k0", ak0o, 1), ("v0", av0o, 1))
            outs1 = (("q1", aq1o, 2), ("k1", ak1o, 2), ("v1", av1o, 2))

            def out_ap(dst, nh, mi, ti):
                r, off = ti // 2, (ti % 2) * 512
                if nh == 1:
                    return dst[r * 128:(r + 1) * 128, off:off + 512]
                return dst[r * 256 + mi * 128:r * 256 + (mi + 1) * 128,
                           off:off + 512]

            xw75 = [None, None]
            for view, outs, ccdep in ((xng_v, outs0, cc_xn),
                                      (xlg_v, outs1, cc_x1)):
                for ti in range(16):
                    sl = ti % 2
                    rr, off = ti // 2, (ti % 2) * 512
                    t_x75 = prog.dma1(
                        "sync",
                        lambda e, o=xbuf[sl][:, :, :],
                               i=view[:, rr, :, off:off + 512]:
                            e.dma_start(o, i), [xw75[sl], ccdep])
                    for nm, dst, nh in outs:
                        for mi in range(nh):
                            bank, war, bi = psum.acquire()
                            mm = None
                            for ki in range(16):
                                mm = prog.op(
                                    "tensor",
                                    lambda e, o=bank[:, :],
                                           l=wsl[nm][:, ki, mi * 128:(mi + 1) * 128],
                                           rx=xbuf[sl][:, ki, :],
                                           s=(ki == 0), p=(ki == 15):
                                        e.matmul(o, l, rx, start=s, stop=p),
                                    [t_w, t_x75] + ([war] if ki == 0 else []),
                                    inc=(ki == 15))
                            tile, twar, si = stage75.acquire()
                            cp = prog.op("scalar",
                                         lambda e, d=tile[:, :], ps=bank[:, :]:
                                             e.activation(d, ps, AF.Copy),
                                         [mm, twar])
                            psum.release(bi, cp)
                            dtok = prog.dma1(
                                "scalar",
                                lambda e, o=out_ap(dst, nh, mi, ti),
                                       i=tile[:, :]:
                                    e.dma_start(o, i), [cp])
                            stage75.release(si, dtok)
                    xw75[sl] = prog.token("tensor")
            prog.barrier()

        with ExitStack() as st:
            sb = lambda n_, sh, dt: st.enter_context(nc.sbuf_tensor(n_, sh, dt))
            vsb = sb("vsb", [PART, 2, 1024], BF16)
            ident = sb("ident", [PART, PART], BF16)
            vout = StagePool(prog, st, "vo", 4, (PART, PART), BF16)
            pstr = st.enter_context(nc.psum_tensor("pstr", [PART, PART], BF16))
            t_idm = prog.op("gpsimd", lambda e: e.iota(
                ident[:, :], pattern=[[1, PART]], base=0, channel_multiplier=-1,
                allow_small_or_imprecise_dtypes=True))
            t_ide = prog.op("vector",
                            lambda e: e.tensor_single_scalar(
                                ident[:, :], ident[:, :], 0.0, ALU.is_equal),
                            [t_idm])
            pwar = None
            for (src, dst, nh, cc_dep) in ((av0o, v0tok, 1, None),
                                           (av1o, v1tok, 2, None)):
                sv = src.rearrange("(r h p) t -> p r h t", p=PART, h=nh)
                for h in range(nh):
                    for kt in range(64):
                        r, co = kt // 8, (kt % 8) * 128
                        sl = kt % 2
                        t_l = prog.dma1(
                            "sync",
                            lambda e, o=vsb[:, sl, 0:128],
                                   i=sv[:, r, h, co:co + 128]:
                                e.dma_start(o, i),
                            [prog.token("tensor"), cc_dep])
                        t_t = prog.op("tensor",
                                      lambda e, o=pstr[:, :], i=vsb[:, sl, 0:128],
                                             d=ident[:, :]:
                                          e.transpose(o, i, d),
                                      [t_l, t_ide, pwar])
                        ot, owar, oi = vout.acquire()
                        t_c = prog.op("vector",
                                      lambda e, o=ot[:, :], i=pstr[:, :]:
                                          e.tensor_copy(o, i), [t_t, owar])
                        pwar = t_c
                        dst_ap = dst[h, kt] if nh == 2 else dst[kt]
                        dtok = prog.dma1("scalar",
                                         lambda e, o=dst_ap, i=ot[:, :]:
                                             e.dma_start(o, i), [t_c])
                        vout.release(oi, dtok)
            prog.barrier()
        t_vtok = prog.token("scalar")

        with ExitStack() as st:
            sb = lambda n_, sh, dt: st.enter_context(nc.sbuf_tensor(n_, sh, dt))
            q0_sb = sb("q0f", [PART, 2, 8, 1024], BF16)
            q1_sb = sb("q1f", [PART, 2, 8, 1024], BF16)
            mw_sb = sb("mwf", [PART, len(pw_), 512], BF16)
            mc_sb = sb("mcf", [PART, len(pc2), 512], BF16)
            env = {
                "kbuf": sb("kbf", [PART, 2, 8, PART], BF16),
                "vbuf": sb("vbf", [PART, 2, 8, PART], BF16),
                "ppool": StagePool(prog, st, "ppf", 4, (PART, 512), BF16),
                "lrow": sb("lrf", [1, 512], F32),
                "bcast": [sb("bc0f", [PART, 512], F32), sb("bc1f", [PART, 512], F32)],
                "ones_bf": sb("ob_f", [PART, 1], BF16),
                "ones_row": sb("or_f", [1, PART], F32),
            }
            t1 = prog.op("vector", lambda e: e.memset(env["ones_bf"][:, :], 1.0))
            t2 = prog.op("vector", lambda e: e.memset(env["ones_row"][:, :], 1.0))
            env["ones_tok"] = t2
            gl = prog.dma_group()
            for hh in range(2):
                gl.dma("sync", lambda e, hh=hh: e.dma_start(
                    q0_sb[:, hh, :, :],
                    aq0o.rearrange("(r h p) t -> p h r t", p=PART, h=2)[:, hh]))
                gl.dma("sync", lambda e, hh=hh: e.dma_start(
                    q1_sb[:, hh, :, :],
                    aq1o.rearrange("(r h p) t -> p h r t", p=PART, h=2)[:, hh]))
            gl.dma("sync", lambda e: e.dma_start(mw_sb[:, :, :], masks_w))
            gl.dma("sync", lambda e: e.dma_start(mc_sb[:, :, :], masks_c))
            t_in2 = gl.seal()
            scale = 1.0 / float(np.sqrt(128.0))

            k0v = ak0o.rearrange("(r p) t -> p r t", p=PART)
            k1v = ak1o.rearrange("(r h p) t -> p h r t", p=PART, h=2)

            def k0_run(hk, a, b):
                r, c0 = a // 8, (a % 8) * 128
                n = b - a
                return k0v[:, r, c0:c0 + n * 128].rearrange("p (n k) -> p n k", k=128)

            def k1_run(hk, a, b):
                r, c0 = a // 8, (a % 8) * 128
                n = b - a
                return k1v[:, hk, r, c0:c0 + n * 128].rearrange("p (n k) -> p n k", k=128)

            out_stage = StagePool(prog, st, "osf", 4, (PART, 512), BF16)
            sink_o0 = []

            def evac_o0(qt, h, ps_o, bc_sb, deps):
                tile, war, si = out_stage.acquire()
                tt = prog.op("vector",
                             lambda e, o=tile[:, :], a=ps_o[:, :], b=bc_sb[:, :]:
                                 e.tensor_mul(o, a, b), deps + [war])
                dtok = prog.dma1("gpsimd",
                                 lambda e, o=o0_in[h * PART:(h + 1) * PART,
                                                   qt * 512:(qt + 1) * 512],
                                        i=tile[:, :]:
                                     e.dma_start(o, i), [tt])
                out_stage.release(si, dtok)
                sink_o0.append(dtok)
                return tt

            emit_attention(prog, psum, env, kind="window", scale=scale,
                           q_sb=None, k_dram=None, v_dram=v0tok[None],
                           masks_sb=mw_sb, mask_tok=t_in2, klists=kw_, tilemap=tw_,
                           q_tok=t_in2, kv_tok=t_vtok, evac_o=evac_o0,
                           k_run_ap=k0_run, nkv=1, block_align=8,
                           q_ap=lambda h, qg: q0_sb[:, h, qg // 1024,
                                                    qg % 1024:qg % 1024 + 512])
            cc_o0 = gather(o0_in, o0g, sink_o0)
            prog.barrier()

            o1_pool = StagePool(prog, st, "o1f", 2, (PART, 512), BF16)
            modp = sb("modpf", [16, T_FULL], F32)
            mod_banks = {}

            def evac_o1(qt, h, ps_o, bc_sb, deps):
                tile, war, si = o1_pool.acquire()
                tt = prog.op("vector",
                             lambda e, o=tile[:, :], a=ps_o[:, :], b=bc_sb[:, :]:
                                 e.tensor_mul(o, a, b), deps + [war])
                if h == 0:
                    mod_banks[qt] = psum.acquire()
                psM, warM, biM = mod_banks[qt]
                g_mm = prog.op(
                    "tensor",
                    lambda e, o=psM[0:16, :], l=g_sb[:, h, :], r=tile[:, :],
                           s=(h == 0):
                        e.matmul(o, l, r, start=s, stop=(h == 1)),
                    [tt] + ([warM] if h == 0 else []))
                o1_pool.release(si, g_mm)
                if h == 1:
                    cp = prog.op("vector",
                                 lambda e, o=modp[:, qt * 512:(qt + 1) * 512],
                                        i=psM[0:16, :]:
                                     e.tensor_copy(o, i), [g_mm])
                    psum.release(biM, cp)
                return tt

            emit_attention(prog, psum, env, kind="causal", scale=scale,
                           q_sb=None, k_dram=None, v_dram=v1tok,
                           masks_sb=mc_sb, mask_tok=t_in2, klists=kc2, tilemap=tc2,
                           q_tok=t_in2, kv_tok=t_vtok, evac_o=evac_o1,
                           k_run_ap=k1_run, nkv=2, block_align=8,
                           q_ap=lambda h, qg: q1_sb[:, h, qg // 1024,
                                                    qg % 1024:qg % 1024 + 512])
            t_stm = prog.dma1("gpsimd",
                              lambda e: e.dma_start(mod_in, modp[:, :]),
                              [prog.token("vector")])
            cc_mod = allreduce(mod_in, mod_out, [t_stm])
            prog.barrier()

        with ExitStack() as st:
            sb = lambda n_, sh, dt: st.enter_context(nc.sbuf_tensor(n_, sh, dt))
            acc = sb("acc3", [PART, 16, TC], F32)
            actb = sb("actb3", [PART, 16, TC], BF16)
            xtile = [sb("xt0f", [PART, 512], F32), sb("xt1f", [PART, 512], F32)]
            wbufA = sb("wbufAf", [PART, 16, 2, 128], BF16)
            wbufB = sb("wbufBf", [PART, 16, 2, 128], BF16)
            wdC = sb("wdCf", [PART, 8, 1024], BF16)
            prodblk = sb("prodblkf", [PART, 8, TC], BF16)
            mrow = sb("mrowf", [16, TC], F32)
            hA = sb("hAf", [16, TC], F32)
            mm_sb = sb("mmf", [16, TC], BF16)
            b_sb = sb("bsbf", [16, 4096], BF16)
            gtmp = [sb("gt0f", [PART, 512], F32), sb("gt1f", [PART, 512], F32)]
            ptile = StagePool(prog, st, "ptf", 2, (PART, 512), BF16)
            l1w_sb = sb("l1wf", [PART, 16], F32)
            l1b_sb = sb("l1bf", [PART, 16], F32)

            xn_view = xn_d.rearrange("(c p) t -> p c t", p=PART)
            x_view = xT.rearrange("(c p) t -> p c t", p=PART)
            h_view = h_d.rearrange("(c p) t -> p c t", p=PART)

            gl = prog.dma_group()
            gl.dma("sync", lambda e: e.dma_start(acc[:, :, :], xn_view))
            gl.dma("sync", lambda e: e.dma_start(l1w_sb[:, :], l1w))
            gl.dma("sync", lambda e: e.dma_start(l1b_sb[:, :], l1b))
            t_in = gl.seal()
            t_bsb = prog.dma1("gpsimd", lambda e: e.dma_start(b_sb[:, :], b_n[:, 0:4096]))

            o0g_v = o0g.rearrange("(c p) t -> p c t", p=PART)
            t_o0l = prog.dma_percore(
                "sync",
                lambda e, rk: [e.dma_start(out=actb[:, ch, :],
                                           in_=o0g_v[:, ch, rk * TC:(rk + 1) * TC])
                               for ch in range(16)],
                16, [cc_o0])
            mt_tok = prog.dma_percore(
                "sync",
                lambda e, rk: [e.dma_start(out=mrow[0:16, :],
                                           in_=mod_out[0:16, rk * TC:(rk + 1) * TC])],
                1, [cc_mod])

            ws0 = WStreamer(prog, wbufA, wo0t, D, D, mc=128, init_war=None)
            xw = [None, None]

            def evac_comb(mi, ti, ps, mm_tok):
                sl = (mi * 2 + ti) % 2
                xt_tok = prog.dma1("scalar",
                                   lambda e, o=xtile[sl][:, :],
                                          i=x_view[:, mi, ti * 512:(ti + 1) * 512]:
                                       e.dma_start(o, i), [xw[sl]])
                t1 = prog.op("vector",
                             lambda e, o=acc[:, mi, ti * 512:(ti + 1) * 512], a=ps,
                                    b=acc[:, mi, ti * 512:(ti + 1) * 512]:
                                 e.tensor_add(o, a, b), [mm_tok, t_in])
                t2 = prog.op("vector",
                             lambda e, o=acc[:, mi, ti * 512:(ti + 1) * 512],
                                    a=xtile[sl][:, :],
                                    b=acc[:, mi, ti * 512:(ti + 1) * 512]:
                                 e.tensor_add(o, a, b), [t1, xt_tok])
                xw[sl] = t2
                return t1

            gemm(prog, psum, M=D, K=D, T=TC, lhsT=ws0.lhsT,
                 rhs=lambda ki, ti: actb[:, ki, ti * 512:(ti + 1) * 512],
                 lhs_dep=ws0.dep, rhs_dep=lambda ki, ti: t_o0l,
                 evac=evac_comb, m_done=ws0.m_done)
            prog.barrier()

            t_h = prog.dma1("sync", lambda e: e.dma_start(actb[:, :, :], h_view))
            a_sb3, t_a3 = load_weight_bf16(prog, st, "a_sb3f", a_n, D, 16)

            def hA_evac(mi, ti, ps, mm_tok):
                return prog.op("vector",
                               lambda e, o=hA[0:16, ti * 512:(ti + 1) * 512], i=ps:
                                   e.tensor_copy(o, i), [mm_tok])

            gemm(prog, psum, M=16, K=D, T=TC,
                 lhsT=lambda mi, ki: a_sb3[:, ki, :],
                 rhs=lambda ki, ti: actb[:, ki, ti * 512:(ti + 1) * 512],
                 lhs_dep=lambda mi, ki: t_a3, rhs_dep=lambda ki, ti: t_h,
                 evac=hA_evac)
            t_tanh = prog.op("scalar",
                             lambda e: e.activation(mrow[0:16, :], mrow[0:16, :],
                                                    AF.Tanh), [mt_tok])
            t_modv = prog.op("vector",
                             lambda e: e.tensor_scalar_mul(mrow[0:16, :],
                                                           mrow[0:16, :], 0.1),
                             [t_tanh])
            t_mm = prog.op("vector",
                           lambda e: e.tensor_mul(mm_sb[0:16, :], mrow[0:16, :],
                                                  hA[0:16, :]),
                           [t_modv, prog.token("vector")])

            wsi = WStreamer(prog, wbufA, w_int, D, 8192, mc=128,
                            init_war=prog.token("tensor"))
            wsu = WStreamer(prog, wbufB, w_upt, D, 8192, mc=128, init_war=None)
            prod_view = prod_d.rearrange("(c p) t -> p c t", p=PART)
            gw = [None, None]
            t_bh = t_bsb
            for mi in range(64):
                if mi == 32:
                    t_bh = prog.dma1("gpsimd",
                                     lambda e: e.dma_start(b_sb[:, :],
                                                           b_n[:, 4096:8192]),
                                     [prog.token("tensor")])
                for ti in range(2):
                    t0 = ti * 512
                    psG, warG, biG = psum.acquire()
                    psU, warU, biU = psum.acquire()
                    mmG = mmU = None
                    for ki in range(16):
                        mmG = prog.op(
                            "tensor",
                            lambda e, o=psG[:, :], l=wsi.lhsT(mi, ki),
                                   r=actb[:, ki, t0:t0 + 512], s=(ki == 0):
                                e.matmul(o, l, r, start=s, stop=False),
                            [wsi.dep(mi, ki), t_h] + ([warG] if ki == 0 else []),
                            inc=False)
                        mmU = prog.op(
                            "tensor",
                            lambda e, o=psU[:, :], l=wsu.lhsT(mi, ki),
                                   r=actb[:, ki, t0:t0 + 512], s=(ki == 0),
                                   p=(ki == 15):
                                e.matmul(o, l, r, start=s, stop=p),
                            [wsu.dep(mi, ki)] + ([warU] if ki == 0 else []),
                            inc=(ki == 15))
                    moff = (mi % 32) * 128
                    mmG = prog.op(
                        "tensor",
                        lambda e, o=psG[:, :], l=b_sb[0:16, moff:moff + 128],
                               r=mm_sb[0:16, t0:t0 + 512]:
                            e.matmul(o, l, r, start=False, stop=True),
                        [t_mm, t_bh])
                    sl = ti
                    t_si = prog.op(
                        "scalar",
                        lambda e, o=gtmp[sl][:, :], i=psG[:, :]:
                            e.activation(o, i, AF.Silu), [mmG, gw[sl]])
                    psum.release(biG, t_si)
                    pt, pwar, pi = ptile.acquire()
                    t_pr = prog.op(
                        "vector",
                        lambda e, o=pt[:, :], a=psU[:, :], b=gtmp[sl][:, :]:
                            e.tensor_mul(o, a, b), [t_si, mmU, pwar])
                    gw[sl] = t_pr
                    psum.release(biU, t_pr)
                    dtok = prog.dma1("gpsimd",
                                     lambda e, o=prod_view[:, mi, t0:t0 + 512],
                                            i=pt[:, :]:
                                         e.dma_start(o, i), [t_pr])
                    ptile.release(pi, dtok)
                wsi.m_done(mi, prog.token("tensor"))
                wsu.m_done(mi, prog.token("tensor"))
            prog.barrier()

            wd_view = w_downt.rearrange("(kc p) m -> p kc m", p=PART)
            for kb in range(8):
                t_pb = prog.dma1("sync",
                                 lambda e, o=prodblk[:, :, :],
                                        i=prod_view[:, kb * 8:(kb + 1) * 8, :]:
                                     e.dma_start(o, i),
                                 [prog.token("tensor") if kb else None])
                for mh in range(2):
                    t_wd = prog.dma1(
                        "gpsimd",
                        lambda e, o=wdC[:, :, :],
                               i=wd_view[:, kb * 8:(kb + 1) * 8,
                                         mh * 1024:(mh + 1) * 1024]:
                            e.dma_start(o, i),
                        [prog.token("tensor")])
                    for mj in range(8):
                        mi = mh * 8 + mj
                        for ti in range(2):
                            t0 = ti * 512
                            bank, war, bi = psum.acquire()
                            mm = None
                            for kj in range(8):
                                mm = prog.op(
                                    "tensor",
                                    lambda e, o=bank[:, :],
                                           l=wdC[:, kj, mj * 128:(mj + 1) * 128],
                                           r=prodblk[:, kj, t0:t0 + 512],
                                           s=(kj == 0), p=(kj == 7):
                                        e.matmul(o, l, r, start=s, stop=p),
                                    [t_wd, t_pb] + ([war] if kj == 0 else []),
                                    inc=(kj == 7))
                            tadd = prog.op(
                                "vector",
                                lambda e, o=acc[:, mi, t0:t0 + 512], a=bank[:, :],
                                       b=acc[:, mi, t0:t0 + 512]:
                                    e.tensor_add(o, a, b), [mm])
                            psum.release(bi, tadd)
            prog.barrier()

            x1_view = x1_d.rearrange("(c p) t -> p c t", p=PART)
            g1_ = prog.dma_group()
            for ch in range(16):
                g1_.dma("sync", lambda e, ch=ch: e.dma_start(
                    x1_view[:, ch, :], acc[:, ch, :]), [prog.token("vector")])
            t_x1o = g1_.seal()
            t_xn2 = layernorm(prog, psum, misc, x=acc, nch=16, T=TC,
                              out_f32=acc, out_bf16=actb, eps=1e-5,
                              use_mean=True, dep=t_x1o)
            xn2_view = xn2_d.rearrange("(c p) t -> p c t", p=PART)
            xn2b_view = xn2b_d.rearrange("(c p) t -> p c t", p=PART)
            g2_ = prog.dma_group()
            for ch in range(16):
                g2_.dma("sync", lambda e, ch=ch: e.dma_start(
                    xn2_view[:, ch, :], acc[:, ch, :]), [t_xn2])
                g2_.dma("scalar", lambda e, ch=ch: e.dma_start(
                    xn2b_view[:, ch, :], actb[:, ch, :]), [t_xn2])
            t_xn2o = g2_.seal()
            t_u = layernorm(prog, psum, misc, x=acc, nch=16, T=TC,
                            out_bf16=actb, gamma=l1w_sb, beta=l1b_sb, eps=1e-5,
                            use_mean=True, dep=t_xn2o)
            u_view = u_in.rearrange("(c p) t -> p c t", p=PART)
            g3_ = prog.dma_group(sw=True)
            for ch in range(16):
                g3_.dma("gpsimd", lambda e, ch=ch: e.dma_start(
                    u_view[:, ch, :], actb[:, ch, :]), [t_u])
            t_uo = g3_.seal()
            cc_u = gather(u_in, ug, [t_uo])
            prog.barrier()

        with ExitStack() as st:
            sb = lambda n_, sh, dt: st.enter_context(nc.sbuf_tensor(n_, sh, dt))
            ip_sb, t_ip = load_weight_bf16(prog, st, "ip_sbf", ipT, D, 1024)
            xp_sb = sb("xp_sbf", [PART, 4, 160], BF16)
            dtw_sb = sb("dtw_sbf", [PART, 512], BF16)
            cw_sb = sb("cw_sbf", [PART, 4, 4], F32)
            cb_sb = sb("cb_sbf", [PART, 4], F32)
            dtb_sb = sb("dtb_sbf", [PART, 4], F32)
            ubuf = [sb("ub0f", [PART, 16, 512], BF16), sb("ub1f", [PART, 16, 512], BF16)]
            xm_sb = sb("xm_sbf", [PART, 4, 515], F32)
            halo = sb("halof", [PART, 4, 3], F32)
            xct = sb("xctf", [PART, 4, 512], F32)
            xcb = sb("xcbf", [PART, 4, 512], BF16)
            ztile = StagePool(prog, st, "ztf", 2, (PART, 512), F32)
            xc_stage = StagePool(prog, st, "xcsf", 2, (PART, 4, 512), F32)
            dtr_sb = sb("dtr_sbf", [PART, T_FULL], BF16)
            xdbl_st = StagePool(prog, st, "xdst", 2, (PART, 512), F32)
            sink_xdbl = []
            spt = sb("sptf", [PART, 2, 512], F32)
            dt_stage = StagePool(prog, st, "dtsf", 2, (PART, 512), F32)

            gsw = prog.dma_group(sw=True)
            gsw.dma("gpsimd", lambda e: e.dma_start(
                xp_sb[:, :, :], xpT.rearrange("(c p) s -> p c s", p=PART)))
            gsw.dma("gpsimd", lambda e: e.dma_start(dtw_sb[:, :], dtwT))
            t_insw = gsw.seal()
            gl = prog.dma_group()
            gl.dma("sync", lambda e: e.dma_start(cw_sb[:, :, :], convw))
            gl.dma("sync", lambda e: e.dma_start(cb_sb[:, :], convb))
            gl.dma("sync", lambda e: e.dma_start(dtb_sb[:, :], dtb))
            t_in = gl.seal()
            t_z0 = prog.op("vector", lambda e: e.memset(halo[:, :, :], 0.0))

            u_view4 = ug.rearrange("(r c p) t -> p r c t", p=PART, c=16)
            uw = [None, None]
            t_halo = t_z0
            xm_war = [None] * 4
            xct_war = [None] * 4
            spt_war = [None, None]
            for tt in range(16):
                t0g = tt * 512
                r_, off_ = tt // 2, (tt % 2) * 512
                sl = tt % 2
                t_u2 = prog.dma1("sync",
                                 lambda e, o=ubuf[sl][:, :, :],
                                        i=u_view4[:, r_, :, off_:off_ + 512]:
                                     e.dma_start(o, i), [uw[sl], cc_u])
                xm_toks = []
                for g in range(8):
                    bank, war, bi = psum.acquire()
                    mm = None
                    for ki in range(16):
                        mm = prog.op(
                            "tensor",
                            lambda e, o=bank[:, :], l=ip_sb[:, ki, g * 128:(g + 1) * 128],
                                   r=ubuf[sl][:, ki, :], s=(ki == 0), p=(ki == 15):
                                e.matmul(o, l, r, start=s, stop=p),
                            [t_insw, t_ip, t_u2] + ([war] if ki == 0 else []),
                            inc=(ki == 15))
                    if g < 4:
                        tk = prog.op("vector",
                                     lambda e, o=xm_sb[:, g, 3:515], i=bank[:, :]:
                                         e.tensor_copy(o, i),
                                     [mm, t_halo, xm_war[g]])
                        xm_toks.append(tk)
                    else:
                        zt_, zwar, zi = ztile.acquire()
                        tk = prog.op("vector",
                                     lambda e, o=zt_[:, :], i=bank[:, :]:
                                         e.tensor_copy(o, i), [mm, zwar])
                        dtok = prog.dma1("gpsimd",
                                         lambda e, o=z_d[(g - 4) * 128:(g - 3) * 128,
                                                         t0g:t0g + 512], i=zt_[:, :]:
                                             e.dma_start(o, i), [tk])
                        ztile.release(zi, dtok)
                    psum.release(bi, tk)
                t_hcopy = prog.op("vector",
                                  lambda e: e.tensor_copy(xm_sb[:, :, 0:3],
                                                          halo[:, :, :]),
                                  xm_toks + [t_z0, t_halo])
                xcs_tile, xcs_war, xcs_i = xc_stage.acquire()
                conv_toks = []
                for db in range(4):
                    tk = prog.op("vector",
                                 lambda e, db=db:
                                     e.tensor_scalar(xct[:, db, :], xm_sb[:, db, 0:512],
                                                     cw_sb[:, db, 0:1], None, ALU.mult),
                                 [xm_toks[db], t_hcopy, t_in, xct_war[db]])
                    for kk in range(1, 4):
                        tk = prog.op("vector",
                                     lambda e, db=db, kk=kk:
                                         e.scalar_tensor_tensor(
                                             xct[:, db, :], xm_sb[:, db, kk:kk + 512],
                                             cw_sb[:, db, kk:kk + 1], xct[:, db, :],
                                             ALU.mult, ALU.add),
                                     [tk])
                    xm_war[db] = tk
                    tk = prog.op("scalar",
                                 lambda e, o=xcs_tile[:, db, :], i=xct[:, db, :],
                                        b=cb_sb[:, db:db + 1]:
                                     e.activation(o, i, AF.Silu, bias=b),
                                 [tk, xcs_war])
                    xct_war[db] = tk
                    tkb = prog.op("vector",
                                  lambda e, o=xcb[:, db, :], i=xcs_tile[:, db, :]:
                                      e.tensor_copy(o, i),
                                  [tk])
                    conv_toks.append(tkb)
                t_halo = prog.op("vector",
                                 lambda e: e.tensor_copy(halo[:, :, :],
                                                         xm_sb[:, :, 512:515]),
                                 xm_toks + [t_hcopy])
                dtok = prog.dma1("sync",
                                 lambda e, o=xc_d.rearrange("(c p) t -> p c t", p=PART)
                                        [:, :, t0g:t0g + 512], i=xcs_tile[:, :, :]:
                                     e.dma_start(o, i), conv_toks)
                xc_stage.release(xcs_i, dtok)
                for (r0, rw) in ((0, PART), (PART, 32)):
                    bank, war, bi = psum.acquire()
                    mm = None
                    for db in range(4):
                        mm = prog.op(
                            "tensor",
                            lambda e, o=bank[0:rw, :], l=xp_sb[:, db, r0:r0 + rw],
                                   r=xcb[:, db, :], s=(db == 0), p=(db == 3):
                                e.matmul(o, l, r, start=s, stop=p),
                            [t_insw, conv_toks[db]] + ([war] if db == 0 else []),
                            inc=(db == 3))
                    xt_, xwar, xi = xdbl_st.acquire()
                    tk = prog.op("vector",
                                 lambda e, o=xt_[0:rw, :], i=bank[0:rw, :]:
                                     e.tensor_copy(o, i), [mm, xwar])
                    psum.release(bi, tk)
                    dst = ccxi[r0:r0 + rw, t0g:t0g + 512]
                    dtok = prog.dma1("sync",
                                     lambda e, o=dst, i=xt_[0:rw, :]:
                                         e.dma_start(o, i), [tk])
                    xdbl_st.release(xi, dtok)
                    sink_xdbl.append(dtok)
                uw[sl] = prog.token("tensor")

            prog.barrier()
            t_ccx = prog.cc(
                lambda e: e.collective_compute(
                    "AllReduce", ALU.add, replica_groups=RG,
                    ins=[ccxi.opt()], outs=[ccxo.opt()]),
                sink_xdbl)
            t_dtr = prog.dma1("gpsimd",
                              lambda e: e.dma_start(dtr_sb[:, :], ccxo[0:PART, :]),
                              [t_ccx])
            for mi in range(4):
                for tt in range(16):
                    t0g = tt * 512
                    bank, war, bi = psum.acquire()
                    mm = prog.op(
                        "tensor",
                        lambda e, o=bank[:, :], l=dtw_sb[:, mi * 128:(mi + 1) * 128],
                               r=dtr_sb[:, t0g:t0g + 512]:
                            e.matmul(o, l, r, start=True, stop=True),
                        [t_insw, t_dtr, war])
                    sl = tt % 2
                    t_v = prog.op("vector",
                                  lambda e, o=spt[:, sl, :], i=bank[:, :],
                                         b=dtb_sb[:, mi:mi + 1]:
                                      e.tensor_scalar_add(o, i, b),
                                  [mm, t_in, spt_war[sl]])
                    psum.release(bi, t_v)
                    dtile, dwar, di = dt_stage.acquire()
                    t_ab = prog.op("scalar",
                                   lambda e, o=dtile[:, :], i=spt[:, sl, :]:
                                       e.activation(o, i, AF.Abs),
                                   [t_v, dwar])
                    t_ex = prog.op("scalar",
                                   lambda e, o=dtile[:, :], i=dtile[:, :]:
                                       e.activation(o, i, AF.Exp, scale=-1.0),
                                   [t_ab])
                    one_ap, one_tok = misc.cc(1.0)
                    t_ln = prog.op("scalar",
                                   lambda e, o=dtile[:, :], i=dtile[:, :], b=one_ap:
                                       e.activation(o, i, AF.Ln, bias=b),
                                   [t_ex, one_tok])
                    t_sp = prog.op("vector",
                                   lambda e, o=dtile[:, :], a=spt[:, sl, :], b=dtile[:, :]:
                                       e.scalar_tensor_tensor(o, a, 0.0, b,
                                                              ALU.max, ALU.add),
                                   [t_ln, t_v])
                    spt_war[sl] = t_sp
                    dtok = prog.dma1("sync",
                                     lambda e, o=dt_d[mi * 128:(mi + 1) * 128,
                                                      t0g:t0g + 512],
                                            i=dtile[:, :]:
                                         e.dma_start(o, i), [t_sp])
                    dt_stage.release(di, dtok)
            prog.barrier()

        with ExitStack() as st:
            sb = lambda n_, sh, dt: st.enter_context(nc.sbuf_tensor(n_, sh, dt))
            LC = 1024
            an = sb("anf", [PART, 4, 16], F32)
            dskb = sb("dskbf", [PART, 4], F32)
            dtc = sb("dtcf", [PART, 4, LC], F32)
            xcc = sb("xccf", [PART, 4, LC], F32)
            uc = sb("ucf", [PART, 4, LC], F32)
            zc = sb("zcf", [PART, 4, LC], F32)
            yacc = sb("yaccf", [PART, 4, LC], F32)
            ygb = sb("ygbf", [PART, 4, LC], BF16)
            dAt = [sb("dA0f", [PART, LC], F32), sb("dA1f", [PART, LC], F32)]
            bt = [sb("bt0f", [PART, LC], F32), sb("bt1f", [PART, LC], F32)]
            ht = [sb("ht0f", [PART, LC], F32), sb("ht1f", [PART, LC], F32)]
            tmp = [sb("tp0f", [PART, LC], F32), sb("tp1f", [PART, LC], F32)]
            Bb = sb("Bbf", [PART, LC], F32)
            Cb = sb("Cbf", [PART, LC], F32)
            brow = [sb("br0f", [1, LC], F32), sb("br1f", [1, LC], F32)]
            crow = [sb("cr0f", [1, LC], F32), sb("cr1f", [1, LC], F32)]
            state = sb("statef", [PART, 64], F32)
            ones_row5 = sb("ones_r5f", [1, PART], F32)

            bcT = ccxo[PART:160, :]
            xc_view = xc_d.rearrange("(c p) t -> p c t", p=PART)
            z_view = z_d.rearrange("(c p) t -> p c t", p=PART)
            dt_view = dt_d.rearrange("(c p) t -> p c t", p=PART)

            t_or = prog.op("vector", lambda e: e.memset(ones_row5[:, :], 1.0))
            t_state = prog.op("vector", lambda e: e.memset(state[:, :], 0.0))
            gl = prog.dma_group()
            gl.dma("sync", lambda e: e.dma_start(an[:, :, :], alog))
            gl.dma("sync", lambda e: e.dma_start(dskb[:, :], dsk))
            t_al = gl.seal()
            t_an = prog.op("scalar", lambda e: e.activation(
                an[:, :, :], an[:, :, :], AF.Exp), [t_al])
            t_an = prog.op("vector", lambda e: e.tensor_scalar_mul(
                an[:, :, :], an[:, :, :], -1.0), [t_an])

            war = {"dA": [None, None], "bt": [None, None], "ht": [None, None],
                   "tmp": [None, None], "brow": [None, None], "crow": [None, None],
                   "Bb": None, "Cb": None, "dtc": None, "xcc": None, "uc": None,
                   "zc": None, "yacc": [None] * 4, "ygb": [None] * 4}
            state_tok = [t_state] * 64
            sink_yg = []
            yg_in_v = yg_in.rearrange("(d p) t -> p d t", p=PART)
            for tc in range(8):
                t0 = tc * LC
                g = prog.dma_group()
                g.dma("sync", lambda e, o=dtc[:, :, :], i=dt_view[:, :, t0:t0 + LC]:
                      e.dma_start(o, i), [war["dtc"]])
                g.dma("sync", lambda e, o=xcc[:, :, :], i=xc_view[:, :, t0:t0 + LC]:
                      e.dma_start(o, i), [war["xcc"]])
                t_ld = g.seal()
                t_u5 = prog.op("vector", lambda e: e.tensor_mul(
                    uc[:, :, :], dtc[:, :, :], xcc[:, :, :]), [t_ld, war["uc"]])
                for s_ in range(16):
                    sl = s_ % 2
                    t_br = prog.dma1("scalar",
                                     lambda e, o=brow[sl][0:1, :],
                                            i=bcT[s_:s_ + 1, t0:t0 + LC]:
                                         e.dma_start(o, i), [war["brow"][sl], t_ccx])
                    t_cr = prog.dma1("scalar",
                                     lambda e, o=crow[sl][0:1, :],
                                            i=bcT[16 + s_:17 + s_, t0:t0 + LC]:
                                         e.dma_start(o, i), [war["crow"][sl], t_ccx])
                    btoks, ctoks = [], []
                    for half in range(2):
                        c0 = half * 512
                        bank, bwar, bi = psum.acquire()
                        mm = prog.op("tensor",
                                     lambda e, o=bank[:, :], l=ones_row5[0:1, :],
                                            r=brow[sl][0:1, c0:c0 + 512]:
                                         e.matmul(o, l, r, start=True, stop=True),
                                     [t_br, t_or, bwar])
                        tk = prog.op("vector",
                                     lambda e, o=Bb[:, c0:c0 + 512], i=bank[:, :]:
                                         e.tensor_copy(o, i), [mm, war["Bb"]])
                        psum.release(bi, tk)
                        btoks.append(tk)
                        bank, bwar, bi = psum.acquire()
                        mm = prog.op("tensor",
                                     lambda e, o=bank[:, :], l=ones_row5[0:1, :],
                                            r=crow[sl][0:1, c0:c0 + 512]:
                                         e.matmul(o, l, r, start=True, stop=True),
                                     [t_cr, t_or, bwar])
                        tk = prog.op("vector",
                                     lambda e, o=Cb[:, c0:c0 + 512], i=bank[:, :]:
                                         e.tensor_copy(o, i), [mm, war["Cb"]])
                        psum.release(bi, tk)
                        ctoks.append(tk)
                    war["brow"][sl] = btoks[-1]
                    war["crow"][sl] = ctoks[-1]
                    b_done = btoks + ctoks
                    for db in range(4):
                        wsl = (s_ * 4 + db) % 2
                        idx = db * 16 + s_
                        t_dA = prog.op("scalar",
                                       lambda e, o=dAt[wsl][:, :], i=dtc[:, db, :],
                                              sc=an[:, db, s_:s_ + 1]:
                                           e.activation(o, i, AF.Exp, scale=sc),
                                       [t_ld, t_an, war["dA"][wsl]])
                        t_b = prog.op("vector",
                                      lambda e, o=bt[wsl][:, :], a=uc[:, db, :], b=Bb[:, :]:
                                          e.tensor_mul(o, a, b),
                                      [t_u5, war["bt"][wsl]] + b_done)
                        t_h5 = prog.op("vector",
                                       lambda e, o=ht[wsl][:, :], a=dAt[wsl][:, :],
                                              b=bt[wsl][:, :], ini=state[:, idx:idx + 1]:
                                           e.tensor_tensor_scan(o, a, b, ini,
                                                                ALU.mult, ALU.add),
                                       [t_dA, t_b, state_tok[idx], war["ht"][wsl]])
                        war["dA"][wsl] = t_h5
                        war["bt"][wsl] = t_h5
                        t_st5 = prog.op("vector",
                                        lambda e, o=state[:, idx:idx + 1],
                                               i=ht[wsl][:, LC - 1:LC]:
                                            e.tensor_copy(o, i), [t_h5])
                        state_tok[idx] = t_st5
                        if s_ == 0:
                            t_y = prog.op("vector",
                                          lambda e, o=yacc[:, db, :], a=ht[wsl][:, :],
                                                 b=Cb[:, :]:
                                              e.tensor_mul(o, a, b),
                                          [t_h5, war["yacc"][db]] + b_done)
                        else:
                            t_m = prog.op("vector",
                                          lambda e, o=tmp[wsl][:, :], a=ht[wsl][:, :],
                                                 b=Cb[:, :]:
                                              e.tensor_mul(o, a, b),
                                          [t_h5, war["tmp"][wsl]] + b_done)
                            t_y = prog.op("vector",
                                          lambda e, o=yacc[:, db, :], a=yacc[:, db, :],
                                                 b=tmp[wsl][:, :]:
                                              e.tensor_add(o, a, b),
                                          [t_m, war["yacc"][db]])
                            war["tmp"][wsl] = t_y
                        war["yacc"][db] = t_y
                        war["ht"][wsl] = t_y
                    war["Bb"] = prog.token("vector")
                    war["Cb"] = war["Bb"]
                t_z5 = prog.dma1("sync",
                                 lambda e, o=zc[:, :, :], i=z_view[:, :, t0:t0 + LC]:
                                     e.dma_start(o, i), [war["zc"]])
                out_toks = []
                for db in range(4):
                    t_yd = prog.op("vector",
                                   lambda e, o=yacc[:, db, :], a=xcc[:, db, :],
                                          sc=dskb[:, db:db + 1], b=yacc[:, db, :]:
                                       e.scalar_tensor_tensor(o, a, sc, b,
                                                              ALU.mult, ALU.add),
                                   [war["yacc"][db], t_al, t_ld])
                    t_sz = prog.op("scalar",
                                   lambda e, o=zc[:, db, :], i=zc[:, db, :]:
                                       e.activation(o, i, AF.Silu),
                                   [t_z5])
                    t_g = prog.op("vector",
                                  lambda e, o=ygb[:, db, :], a=yacc[:, db, :],
                                         b=zc[:, db, :]:
                                      e.tensor_mul(o, a, b),
                                  [t_yd, t_sz, war["ygb"][db]])
                    out_toks.append(t_g)
                    war["yacc"][db] = t_g
                t_out5 = prog.dma1("gpsimd",
                                   lambda e, o=yg_in_v[:, :, t0:t0 + LC],
                                          i=ygb[:, :, :]:
                                       e.dma_start(o, i), out_toks)
                sink_yg.append(t_out5)
                for db in range(4):
                    war["ygb"][db] = t_out5
                war["zc"] = prog.token("vector")
                war["dtc"] = prog.token("vector")
                war["xcc"] = prog.token("vector")
                war["uc"] = prog.token("vector")
            cc_yg = gather(yg_in, ygall, sink_yg)
            prog.barrier()

        og = sbo("og_f", [PART, 16, TC], F32)
        with ExitStack() as st:
            sb = lambda n_, sh, dt: st.enter_context(nc.sbuf_tensor(n_, sh, dt))
            h2b = sb("h2bf", [PART, 16, TC], BF16)
            l2w_sb = sb("l2wf", [PART, 16], F32)
            l2b_sb = sb("l2bf", [PART, 16], F32)
            xn2_view = xn2_d.rearrange("(c p) t -> p c t", p=PART)
            yg_view = ygall.rearrange("(kc p) t -> p kc t", p=PART)
            hsw_view = hsw_d.rearrange("(c p) t -> p c t", p=PART)
            gl = prog.dma_group()
            gl.dma("sync", lambda e: e.dma_start(og[:, :, :], xn2_view))
            gl.dma("sync", lambda e: e.dma_start(l2w_sb[:, :], l2w))
            gl.dma("sync", lambda e: e.dma_start(l2b_sb[:, :], l2b))
            t_in6 = gl.seal()

            def kblocked_acc6(wt_dram, rhs_view, acc_dep, tag, rhs_dep=None,
                              percore_cols=False):
                kpb = 8
                wview = wt_dram.rearrange("(kc p) m -> p kc m", p=PART)
                with ExitStack() as st2:
                    rhsblk = st2.enter_context(
                        nc.sbuf_tensor(f"kbr{tag}", [PART, kpb, TC], BF16))
                    wblk = st2.enter_context(
                        nc.sbuf_tensor(f"kbw{tag}", [PART, kpb, 1024], BF16))
                    for kb in range(4):
                        if percore_cols:
                            t_r = prog.dma_percore(
                                "sync",
                                lambda e, rk, kb=kb: [e.dma_start(
                                    out=rhsblk[:, :, :],
                                    in_=rhs_view[:, kb * kpb:(kb + 1) * kpb,
                                                 rk * TC:(rk + 1) * TC])],
                                1,
                                [prog.token("tensor") if kb else acc_dep,
                                 rhs_dep])
                        else:
                            t_r = prog.dma1(
                                "sync",
                                lambda e, o=rhsblk[:, :, :],
                                       i=rhs_view[:, kb * kpb:(kb + 1) * kpb, :]:
                                    e.dma_start(o, i),
                                [prog.token("tensor") if kb else acc_dep,
                                 rhs_dep])
                        for mh in range(2):
                            t_w = prog.dma1(
                                "gpsimd",
                                lambda e, o=wblk[:, :, :],
                                       i=wview[:, kb * kpb:(kb + 1) * kpb,
                                               mh * 1024:(mh + 1) * 1024]:
                                    e.dma_start(o, i),
                                [prog.token("tensor")])
                            for mj in range(8):
                                mi = mh * 8 + mj
                                for ti in range(2):
                                    t0 = ti * 512
                                    bank, war, bi = psum.acquire()
                                    mm = None
                                    for kj in range(kpb):
                                        mm = prog.op(
                                            "tensor",
                                            lambda e, o=bank[:, :],
                                                   l=wblk[:, kj, mj * 128:(mj + 1) * 128],
                                                   r=rhsblk[:, kj, t0:t0 + 512],
                                                   s=(kj == 0), p=(kj == kpb - 1):
                                                e.matmul(o, l, r, start=s, stop=p),
                                            [t_w, t_r] + ([war] if kj == 0 else []),
                                            inc=(kj == kpb - 1))
                                    tadd = prog.op(
                                        "vector",
                                        lambda e, o=og[:, mi, t0:t0 + 512], a=bank[:, :],
                                               b=og[:, mi, t0:t0 + 512]:
                                            e.tensor_add(o, a, b), [mm, t_in6])
                                    psum.release(bi, tadd)
                    prog.barrier()

            kblocked_acc6(opT, yg_view, None, "m6", rhs_dep=cc_yg,
                          percore_cols=True)
            t_h2 = layernorm(prog, psum, misc, x=og, nch=16, T=TC,
                             out_bf16=h2b, gamma=l2w_sb, beta=l2b_sb, eps=1e-5,
                             use_mean=True, dep=prog.token("vector"))
            with ExitStack() as st2:
                wba = st2.enter_context(nc.sbuf_tensor("wbaf", [PART, 16, 2, 128], BF16))
                wbg = st2.enter_context(nc.sbuf_tensor("wbgf", [PART, 16, 2, 128], BF16))
                gt6 = st2.enter_context(nc.sbuf_tensor("gt6f", [PART, 2, 512], F32))
                wsa = WStreamer(prog, wba, ffpT[:, 0:4096], D, 4096, mc=128,
                                init_war=None)
                wsg2 = WStreamer(prog, wbg, ffpT[:, 4096:8192], D, 4096, mc=128,
                                 init_war=None)
                hstage = StagePool(prog, st2, "hst6f", 2, (PART, 512), BF16)
                gw = [None, None]
                for mj in range(32):
                    for ti in range(2):
                        t0 = ti * 512
                        psA, warA, biA = psum.acquire()
                        psG, warG, biG = psum.acquire()
                        mmA = mmG = None
                        for ki in range(16):
                            mmA = prog.op(
                                "tensor",
                                lambda e, o=psA[:, :], l=wsa.lhsT(mj, ki),
                                       r=h2b[:, ki, t0:t0 + 512], s=(ki == 0),
                                       p=(ki == 15):
                                    e.matmul(o, l, r, start=s, stop=p),
                                [wsa.dep(mj, ki), t_h2] + ([warA] if ki == 0 else []),
                                inc=(ki == 15))
                            mmG = prog.op(
                                "tensor",
                                lambda e, o=psG[:, :], l=wsg2.lhsT(mj, ki),
                                       r=h2b[:, ki, t0:t0 + 512], s=(ki == 0),
                                       p=(ki == 15):
                                    e.matmul(o, l, r, start=s, stop=p),
                                [wsg2.dep(mj, ki)] + ([warG] if ki == 0 else []),
                                inc=(ki == 15))
                        sl = ti
                        t_si = prog.op("scalar",
                                       lambda e, o=gt6[:, sl, :], i=psG[:, :]:
                                           e.activation(o, i, AF.Silu),
                                       [mmG, gw[sl]])
                        psum.release(biG, t_si)
                        ht_, hwar, hi = hstage.acquire()
                        t_pr = prog.op("vector",
                                       lambda e, o=ht_[:, :], a=psA[:, :],
                                              b=gt6[:, sl, :]:
                                           e.tensor_mul(o, a, b),
                                       [t_si, mmA, hwar])
                        gw[sl] = t_pr
                        psum.release(biA, t_pr)
                        dtok = prog.dma1("gpsimd",
                                         lambda e, o=hsw_view[:, mj, t0:t0 + 512],
                                                i=ht_[:, :]:
                                             e.dma_start(o, i), [t_pr])
                        hstage.release(hi, dtok)
                    wsa.m_done(mj, prog.token("tensor"))
                    wsg2.m_done(mj, prog.token("tensor"))
            prog.barrier()
            kblocked_acc6(ffoT, hsw_view, None, "f6")

        with ExitStack() as st:
            sb = lambda n_, sh, dt: st.enter_context(nc.sbuf_tensor(n_, sh, dt))
            H = 5632
            HC = 44
            xnb = sb("xnb7f", [PART, 16, TC], BF16)
            ones_col7 = sb("oc7f", [PART, 1], F32)
            ones_row7 = sb("or7f", [1, PART], F32)
            redc = sb("redcf", [PART, 80], F32)
            ths = sb("thsf", [PART, 8], F32)
            t_o1 = prog.op("vector", lambda e: e.memset(ones_col7[:, :], 1.0))
            t_o2 = prog.op("vector", lambda e: e.memset(ones_row7[:, :], 1.0))
            gl = prog.dma_group()
            gl.dma("sync", lambda e: e.dma_start(
                xnb[:, :, :], xn2b_d.rearrange("(c p) t -> p c t", p=PART)))
            t_in7 = gl.seal()
            views = {
                "w1": (w1t.rearrange("(kc p) m -> p kc m", p=PART), 16, H),
                "w2": (w2t.rearrange("(kc p) m -> p kc m", p=PART), 16, H),
                "w3": (w3t.rearrange("(kc p) m -> p kc m", p=PART), HC, D),
            }
            numel = {"w1": D * H, "w2": D * H, "w3": D * H}
            red_cols = {}
            col_i = 0
            with ExitStack() as st2:
                gbuf = [st2.enter_context(nc.sbuf_tensor(f"gbf{i}", [PART, H], F32))
                        for i in range(2)]
                gwar = [None, None]
                for wname in ("w1", "w2", "w3"):
                    view, kc_w, M = views[wname]
                    cols = []
                    for ki in range(kc_w):
                        slq = col_i % 2
                        t_l = prog.dma1("sync",
                                        lambda e, o=gbuf[slq][:, 0:M], i=view[:, ki, :]:
                                            e.dma_start(o, i), [gwar[slq]])
                        t_r = prog.op("vector",
                                      lambda e, o=redc[:, col_i:col_i + 1],
                                             i=gbuf[slq][:, 0:M]:
                                          e.tensor_reduce(o, i, AX, ALU.add,
                                                          apply_absolute_value=True),
                                      [t_l])
                        gwar[slq] = t_r
                        cols.append(col_i)
                        col_i += 1
                    red_cols[wname] = cols
                for wi, wname in enumerate(("w1", "w2", "w3")):
                    cols = red_cols[wname]
                    bank, war, bi = psum.acquire()
                    mm = prog.op("tensor",
                                 lambda e, o=bank[0:1, 0:len(cols)], l=ones_col7[:, :],
                                        r=redc[:, cols[0]:cols[-1] + 1]:
                                     e.matmul(o, l, r, start=True, stop=True),
                                 [prog.token("vector"), t_o1, war])
                    t_s = prog.op("vector",
                                  lambda e, o=redc[0:1, 76 + wi:77 + wi],
                                         i=bank[0:1, 0:len(cols)]:
                                      e.tensor_reduce(o, i, AX, ALU.add), [mm])
                    psum.release(bi, t_s)
                    t_t = prog.op("vector",
                                  lambda e, o=redc[0:1, 76 + wi:77 + wi],
                                         i=redc[0:1, 76 + wi:77 + wi],
                                         s1=1.0 / numel[wname], s2=1e-8:
                                      e.tensor_scalar(o, i, s1, s2, ALU.mult, ALU.add),
                                  [t_s])
                    t_t = prog.op("vector",
                                  lambda e, o=redc[0:1, 76 + wi:77 + wi],
                                         i=redc[0:1, 76 + wi:77 + wi]:
                                      e.tensor_scalar_mul(o, i, 0.5), [t_t])
                    bank, war, bi = psum.acquire()
                    mm = prog.op("tensor",
                                 lambda e, o=bank[:, 0:1], l=ones_row7[0:1, :],
                                        r=redc[0:1, 76 + wi:77 + wi]:
                                     e.matmul(o, l, r, start=True, stop=True),
                                 [t_t, t_o2, war])
                    t_c = prog.op("vector",
                                  lambda e, o=ths[:, 2 * wi:2 * wi + 1], i=bank[:, 0:1]:
                                      e.tensor_copy(o, i), [mm])
                    t_c2 = prog.op("vector",
                                   lambda e, o=ths[:, 2 * wi + 1:2 * wi + 2],
                                          i=ths[:, 2 * wi:2 * wi + 1]:
                                       e.tensor_scalar_mul(o, i, -1.0), [t_c])
                    psum.release(bi, t_c2)
            prog.barrier()
            th_tok = prog.token("vector")

            def quantize7(dst, src, wi, deps):
                t1 = prog.op("vector",
                             lambda e, o=dst, i=src, s=ths[:, 2 * wi + 1:2 * wi + 2]:
                                 e.tensor_scalar(o, i, s, None, ALU.is_lt),
                             deps + [th_tok])
                t2 = prog.op("vector",
                             lambda e, o=dst, i=src, s=ths[:, 2 * wi:2 * wi + 1], b=dst:
                                 e.scalar_tensor_tensor(o, i, s, b, ALU.is_gt,
                                                        ALU.subtract),
                             [t1])
                return t2

            hid_view = hid_d.rearrange("(c p) t -> p c t", p=PART)
            with ExitStack() as st3:
                wf = [st3.enter_context(nc.sbuf_tensor(f"wff{i}", [PART, 16, 128], F32))
                      for i in range(2)]
                qb = [st3.enter_context(nc.sbuf_tensor(f"qbf{i}", [PART, 16, 128], BF16))
                      for i in range(2)]
                sg7 = st3.enter_context(nc.sbuf_tensor("sg7f", [PART, 2, 512], F32))
                hstage = StagePool(prog, st3, "hst7f", 2, (PART, 512), BF16)
                wfwar = [None, None]
                qbwar = [None, None]
                sgwar = [None, None]
                for mj in range(HC):
                    q_toks = {}
                    for wi, wname in enumerate(("w1", "w2")):
                        slq = wi
                        view = views[wname][0]
                        t_l = prog.dma1("sync",
                                        lambda e, o=wf[slq][:, :, :],
                                               i=view[:, :, mj * 128:(mj + 1) * 128]:
                                            e.dma_start(o, i), [wfwar[slq]])
                        t_q = quantize7(qb[slq][:, :, :], wf[slq][:, :, :], wi,
                                        [t_l, qbwar[slq]])
                        wfwar[slq] = t_q
                        q_toks[wi] = t_q
                    for ti in range(2):
                        t0 = ti * 512
                        ps1, war1, bi1 = psum.acquire()
                        ps2, war2, bi2 = psum.acquire()
                        mm1 = mm2 = None
                        for ki in range(16):
                            mm1 = prog.op(
                                "tensor",
                                lambda e, o=ps1[:, :], l=qb[0][:, ki, :],
                                       r=xnb[:, ki, t0:t0 + 512], s=(ki == 0),
                                       p=(ki == 15):
                                    e.matmul(o, l, r, start=s, stop=p),
                                [q_toks[0], t_in7] + ([war1] if ki == 0 else []),
                                inc=(ki == 15))
                            mm2 = prog.op(
                                "tensor",
                                lambda e, o=ps2[:, :], l=qb[1][:, ki, :],
                                       r=xnb[:, ki, t0:t0 + 512], s=(ki == 0),
                                       p=(ki == 15):
                                    e.matmul(o, l, r, start=s, stop=p),
                                [q_toks[1]] + ([war2] if ki == 0 else []),
                                inc=(ki == 15))
                        t_sg = prog.op("scalar",
                                       lambda e, o=sg7[:, ti, :], i=ps1[:, :]:
                                           e.activation(o, i, AF.Sigmoid),
                                       [mm1, sgwar[ti]])
                        psum.release(bi1, t_sg)
                        ht_, hwar, hi = hstage.acquire()
                        t_h7 = prog.op("vector",
                                       lambda e, o=ht_[:, :], a=ps2[:, :],
                                              b=sg7[:, ti, :]:
                                           e.tensor_mul(o, a, b),
                                       [t_sg, mm2, hwar])
                        sgwar[ti] = t_h7
                        psum.release(bi2, t_h7)
                        dtok = prog.dma1("gpsimd",
                                         lambda e, o=hid_view[:, mj, t0:t0 + 512],
                                                i=ht_[:, :]:
                                             e.dma_start(o, i), [t_h7])
                        hstage.release(hi, dtok)
                    qbwar[0] = prog.token("tensor")
                    qbwar[1] = qbwar[0]
            prog.barrier()
            with ExitStack() as st4:
                hblk = st4.enter_context(nc.sbuf_tensor("hblkf", [PART, 11, TC], BF16))
                w3f = st4.enter_context(nc.sbuf_tensor("w3ff", [PART, 11, 512], F32))
                q3b = st4.enter_context(nc.sbuf_tensor("q3bf", [PART, 11, 512], BF16))
                v3 = views["w3"][0]
                for kb in range(4):
                    t_hb = prog.dma1("sync",
                                     lambda e, o=hblk[:, :, :],
                                            i=hid_view[:, kb * 11:(kb + 1) * 11, :]:
                                         e.dma_start(o, i),
                                     [prog.token("tensor") if kb else None])
                    for mh in range(4):
                        t_w3 = prog.dma1(
                            "scalar",
                            lambda e, o=w3f[:, :, :],
                                   i=v3[:, kb * 11:(kb + 1) * 11,
                                        mh * 512:(mh + 1) * 512]:
                                e.dma_start(o, i),
                            [prog.token("vector"), prog.token("tensor")])
                        t_q3 = quantize7(q3b[:, :, :], w3f[:, :, :], 2, [t_w3])
                        for mj in range(4):
                            mi = mh * 4 + mj
                            for ti in range(2):
                                t0 = ti * 512
                                bank, war, bi = psum.acquire()
                                mm = None
                                for kj in range(11):
                                    mm = prog.op(
                                        "tensor",
                                        lambda e, o=bank[:, :],
                                               l=q3b[:, kj, mj * 128:(mj + 1) * 128],
                                               r=hblk[:, kj, t0:t0 + 512],
                                               s=(kj == 0), p=(kj == 10):
                                            e.matmul(o, l, r, start=s, stop=p),
                                        [t_q3, t_hb] + ([war] if kj == 0 else []),
                                        inc=(kj == 10))
                                tadd = prog.op(
                                    "vector",
                                    lambda e, o=og[:, mi, t0:t0 + 512], a=bank[:, :],
                                           b=og[:, mi, t0:t0 + 512]:
                                        e.tensor_add(o, a, b), [mm])
                                psum.release(bi, tadd)
            prog.barrier()
            with ExitStack() as st5:
                xst = [st5.enter_context(nc.sbuf_tensor(f"xstf{i}", [PART, TC], F32))
                       for i in range(2)]
                x1_view7 = x1_d.rearrange("(c p) t -> p c t", p=PART)
                xn2v = xn2_d.rearrange("(c p) t -> p c t", p=PART)
                y_view = o_y.rearrange("(c p) t -> p c t", p=PART)
                xw7 = [None, None]
                for ch in range(16):
                    slx = ch % 2
                    t_x = prog.dma1("sync",
                                    lambda e, o=xst[slx][:, :], i=x1_view7[:, ch, :]:
                                        e.dma_start(o, i), [xw7[slx]])
                    t_a7 = prog.op("vector",
                                   lambda e, o=og[:, ch, :], a=xst[slx][:, :],
                                          b=og[:, ch, :]:
                                       e.tensor_add(o, a, b),
                                   [t_x, prog.token("vector")])
                    xw7[slx] = t_a7
                    prog.dma1("sync",
                              lambda e, o=y_view[:, ch, :], i=og[:, ch, :]:
                                  e.dma_start(o, i), [t_a7])
        prog.barrier()
        block = stack.enter_context(nc.Block())
        prog.install(block)
    return nc


def kernel_fused(**inp):
    x = inp["L0_attn_x"][0]
    kw_, tw_, pw_ = mask_patterns("window")
    kc2, tc2, pc2 = mask_patterns("causal")
    mw_in = _ct(np.stack(list(pw_), axis=1))
    mc_in = _ct(np.stack(list(pc2), axis=1))
    ipT = _ct(inp["L1_attn_in_proj_w"].T)
    nc = build_fused()
    maps = []
    for c in range(NCORES):
        sl = slice(c * TC, (c + 1) * TC)
        cs = slice(512 * c, 512 * (c + 1))
        hs = slice(256 * c, 256 * (c + 1))
        kvs = slice(128 * (c // 2), 128 * (c // 2) + 128)
        maps.append({
            "xT": _ct(x[sl].T),
            "wq0sl": _ct(inp["L0_attn_Wq"].T[:, hs]),
            "wk0sl": _ct(inp["L0_attn_Wk"].T[:, kvs]),
            "wv0sl": _ct(inp["L0_attn_Wv"].T[:, kvs]),
            "wq1sl": _ct(inp["L0_ffn_Wq"].T[:, hs]),
            "wk1sl": _ct(inp["L0_ffn_Wk"].T[:, hs]),
            "wv1sl": _ct(inp["L0_ffn_Wv"].T[:, hs]),
            "wo1sl": _ct(inp["L0_ffn_Wo"][:, 256 * c:256 * (c + 1)]),
            "a_n": inp["L0_ffn_A"],
            "ln1v": _ct(inp["L0_ffn_ln1_w"].reshape(16, 128).T),
            "ln2v": _ct(inp["L0_ffn_ln2_w"].reshape(16, 128).T),
            "masks_w": mw_in, "masks_c": mc_in,
            "wo0t": _ct(inp["L0_attn_Wo"].T),
            "w_int": _ct(inp["L0_ffn_W_in_base"].T),
            "w_upt": _ct(inp["L0_ffn_W_up_base"].T),
            "w_downt": _ct(inp["L0_ffn_W_down"].T),
            "b_n": inp["L0_ffn_B"],
            "l1w": _ct(inp["L1_attn_ln1_w"].reshape(16, 128).T),
            "l1b": _ct(inp["L1_attn_ln1_b"].reshape(16, 128).T),
            "ipT": _ct(np.concatenate(
                [ipT[:, cs], ipT[:, 4096 + 512 * c:4096 + 512 * (c + 1)]], 1)),
            "convw": _ct(inp["L1_attn_conv_w"][cs, 0, :]
                         .reshape(4, 128, 4).transpose(1, 0, 2)),
            "convb": _ct(inp["L1_attn_conv_b"][cs].reshape(4, 128).T),
            "xpT": _ct(inp["L1_attn_x_proj_w"].T[cs]),
            "dtwT": _ct(inp["L1_attn_dt_proj_w"].T[:, cs]),
            "dtb": _ct(inp["L1_attn_dt_proj_b"][cs].reshape(4, 128).T),
            "alog": _ct(inp["L1_attn_A_log"][cs].reshape(4, 128, 16)
                        .transpose(1, 0, 2)),
            "dsk": _ct(inp["L1_attn_D_skip"][cs].reshape(4, 128).T),
            "opT": _ct(inp["L1_attn_out_proj_w"].T),
            "ffpT": _ct(inp["L1_attn_ff_proj_w"].T),
            "ffoT": _ct(inp["L1_attn_ff_out_w"].T),
            "l2w": _ct(inp["L1_attn_ln2_w"].reshape(16, 128).T),
            "l2b": _ct(inp["L1_attn_ln2_b"].reshape(16, 128).T),
            "w1t": _ct(inp["L1_ffn_w1"].T), "w2t": _ct(inp["L1_ffn_w2"].T),
            "w3t": _ct(inp["L1_ffn_w3"].T),
        })
    res = run_bass_kernel_spmd(nc, maps, core_ids=list(range(NCORES))).results
    y = np.concatenate([r["o_y"] for r in res], axis=1)
    return _ct(y.T)[None]


kernel = kernel_fused
